# version 04
# baseline (speedup 1.0000x reference)
import numpy as np
import concourse.bass as bass
import concourse.mybir as mybir
from concourse.bass_utils import run_bass_kernel_spmd
from contextlib import ExitStack

F32 = mybir.dt.float32
BF16 = mybir.dt.bfloat16
I32 = mybir.dt.int32
AF = mybir.ActivationFunctionType
ALU = mybir.AluOpType

P = 128
T = 1024
C = 16
D = 2048
HID = 5632
HC = 44
SEGH = 22
NSEG = 2
NB = 512
EPSV = 1e-5

OFF_SM = 0
OFF_SM2 = 736
OFF_QS = 800
OFF_MBOX = 816
OFF_MG = 944
OFF_CARRY = 1456
OFF_CT1 = 1520
OFF_CT2 = 1584
OFF_ONESR = 1648
OFF_ONESRB = 2160
OFF_ONESC = 2416
OFF_ONESCB = 2420
OFF_EPS = 2424
OFF_CROWD = 4480
OFF_H = 6656
OFF_BIG = 39424
OFF_WS = 137728
OFF_A1 = 162304
OFF_A2 = 166400
OFF_A3 = 170496
OFF_A4 = 174592
OFF_A5 = 178688
OFF_A6 = 182784
OFF_A7 = 186880
TOTAL_BYTES = 207360

SM_BIN = 0
SM_BDEC = 16
SM_DBIAS = 32
SM_BGATE = 48
SM_BOUT = 64
SM_SDOWN = 80
SM_SGATE = 96
SM_SUP = 140


def build_program():
    nc = bass.Bass("TRN2", target_bir_lowering=False)

    xq = nc.dram_tensor("xq", [D, T], F32, kind="ExternalInput").ap()
    w_in = nc.dram_tensor("w_in", [C, P, 2048], F32, kind="ExternalInput").ap()
    w_dec = nc.dram_tensor("w_dec", [C, P, 2048], F32, kind="ExternalInput").ap()
    w_gate = nc.dram_tensor("w_gate", [C, P, 2048], F32, kind="ExternalInput").ap()
    w_out = nc.dram_tensor("w_out", [C, P, 2048], F32, kind="ExternalInput").ap()
    wg_t = nc.dram_tensor("wg_t", [HC, P, 2048], I32, kind="ExternalInput").ap()
    wu_t = nc.dram_tensor("wu_t", [HC, P, 2048], I32, kind="ExternalInput").ap()
    wd_t = nc.dram_tensor("wd_t", [NSEG, C, P, SEGH * 128], I32,
                          kind="ExternalInput").ap()
    smalls = nc.dram_tensor("smalls", [P, 184], F32, kind="ExternalInput").ap()
    qsel = nc.dram_tensor("qsel", [P, 4], F32, kind="ExternalInput").ap()
    outT = nc.dram_tensor("outT", [D, T], F32, kind="ExternalOutput").ap()
    cc_in = nc.dram_tensor("cc_in", [P, 32], F32)
    cc_out = nc.dram_tensor("cc_out", [4 * P, 32], F32)

    xq3 = xq.rearrange("(c p) t -> p c t", p=P)
    out3 = outT.rearrange("(c p) t -> p c t", p=P)
    ccg = cc_out.rearrange("(r p) j -> p r j", p=P)

    with ExitStack() as ctx:
        AR = ctx.enter_context(nc.sbuf_tensor("AR", [P, TOTAL_BYTES // 2], BF16))

        def cv_f32(off_b, rows, cols):
            v = AR[:, off_b // 2:(off_b + cols * 4) // 2].bitcast(F32)
            return v[0:rows, :] if rows != P else v

        def cv_bf(off_b, rows, cols):
            v = AR[:, off_b // 2:(off_b + cols * 2) // 2]
            return v[0:rows, :] if rows != P else v

        SM = cv_f32(OFF_SM, P, 184)
        SM2 = cv_f32(OFF_SM2, P, 16)
        QS = cv_f32(OFF_QS, P, 4)
        MBOX = cv_f32(OFF_MBOX, P, 32)
        MG = cv_f32(OFF_MG, P, 128)
        CARRY = cv_f32(OFF_CARRY, P, 16)
        CT1 = cv_f32(OFF_CT1, P, 16)
        CT2 = cv_f32(OFF_CT2, P, 16)
        ONESR = cv_f32(OFF_ONESR, 1, 128)
        ONESRB = cv_bf(OFF_ONESRB, 1, 128)
        ONESCB = cv_bf(OFF_ONESCB, P, 1)
        EPS = cv_f32(OFF_EPS, 1, 1)
        CROWD = cv_bf(OFF_CROWD, 1, 1024)
        H = cv_bf(OFF_H, P, C * T)
        XN = cv_f32(OFF_BIG, P, C * T)

        def z1t(c):
            return cv_bf(OFF_BIG + c * 2048, P, T)

        def z2t(c):
            return cv_bf(OFF_BIG + 32768 + c * 2048, P, T)

        def yt(c):
            return cv_bf(OFF_BIG + 65536 + c * 2048, P, T)

        def gut(j):
            if j < 16:
                return yt(j)
            return cv_bf(OFF_A7 + (j - 16) * 2048, P, T)

        WS = [cv_bf(OFF_WS + j * 4096, P, 2048) for j in range(6)]
        WD = [cv_bf(OFF_A3, P, SEGH * 128), cv_bf(OFF_A5, P, SEGH * 128)]
        XS = [cv_f32(OFF_A1, P, T), cv_f32(OFF_A2, P, T), cv_f32(OFF_A4, P, T)]
        OUTB = XS
        SQB = [cv_bf(OFF_A3, P, T), cv_bf(OFF_A3 + 2048, P, T)]
        ABC = cv_f32(OFF_A3, P, T)
        BBC = cv_f32(OFF_A4, P, T)
        SCRD = cv_f32(OFF_A3, P, T)
        SCRB = cv_f32(OFF_A4, P, T)
        SCRT = cv_f32(OFF_A5, P, T)
        ZEROS = cv_f32(OFF_A6, P, T)
        STATES = cv_f32(OFF_A5, P, T)
        GPS = cv_f32(OFF_A7, P, T)
        T1 = cv_f32(OFF_A7, 1, 1024)
        T2 = cv_f32(OFF_A7 + 4096, 1, 1024)
        ROW3 = cv_f32(OFF_A7 + 8192, 1, 1024)
        AROW = cv_f32(OFF_A7 + 12288, 1, 1024)
        BROW = cv_f32(OFF_A7 + 16384, 1, 1024)
        GTMP = cv_f32(OFF_A7 + 12288, P, T)

        PS = [ctx.enter_context(nc.psum_tensor(f"PS_{i}", [P, 1024], F32))
              for i in range(4)]

        sem_names = (["ld", "mb", "cc", "mg", "pe", "dv", "ac", "gp",
                      "xs0", "xs1", "xs2", "wd0", "wd1", "so0", "so1"]
                     + [f"ws{j}" for j in range(6)])
        SEMS = {n: ctx.enter_context(nc.semaphore(n)) for n in sem_names}
        CNT = {n: 0 for n in sem_names}

        SCHED = {"sync": [], "gpsimd": [], "vector": [], "scalar": [],
                 "tensor": []}

        def emit(eng, fn):
            SCHED[eng].append(fn)

        def wait(eng, sem, val):
            if val <= 0:
                return
            emit(eng, lambda e, h=SEMS[sem], v=val: e.wait_ge(h, v))

        def op(eng, meth, *args, inc=None, incby=1, **kw):
            if eng in ("vector", "gpsimd") and meth != "dma_start":
                esem = "dv" if eng == "vector" else "gp"
                wait(eng, esem, CNT[esem])
                if inc is None:
                    inc, incby = esem, 1
            if inc:
                CNT[inc] += incby
            h = SEMS[inc] if inc else None
            def fn(e, meth=meth, args=args, kw=kw, h=h, incby=incby):
                r = getattr(e, meth)(*args, **kw)
                if h is not None:
                    r.then_inc(h, incby)
            emit(eng, fn)
            return CNT[inc] if inc else None

        def dma(queue, out, in_, sem):
            return op(queue, "dma_start", inc=sem, incby=16, out=out, in_=in_)

        ws_free = [("pe", 0)] * 6
        ws_ready = [0] * 6

        def load_stripe(j, src_ap):
            s, v = ws_free[j]
            wait("gpsimd", s, v)
            ws_ready[j] = dma("gpsimd", WS[j][:, :], src_ap, f"ws{j}")

        dma("sync", SM, smalls, "ld")
        dma("sync", QS, qsel, "ld")
        op("vector", "memset", ONESR, 1.0)
        op("vector", "memset", ONESRB, 1.0)
        op("vector", "memset", ONESCB, 1.0)
        op("vector", "memset", EPS, EPSV)
        op("vector", "memset", ZEROS, 0.0)
        wait("vector", "ld", 32)
        op("vector", "tensor_tensor", SM2, SM[:, 16:32], SM[:, 32:48], ALU.add)

        ph0_done = [0] * C
        cast_done = [0] * C
        sq_done = [0] * C
        xs_ready = [0, 0, 0]
        for c in range(C):
            j = c % 3
            if c >= 3:
                wait("sync", "dv", cast_done[c - 3])
            xs_ready[j] = dma("sync", XS[j], xq3[:, c, :], f"xs{j}")
            wait("vector", f"xs{j}", xs_ready[j])
            cast_done[c] = op("vector", "tensor_copy", H[:, c * T:(c + 1) * T],
                              XS[j], inc="dv")
            wait("scalar", "dv", cast_done[c])
            if c >= 2:
                wait("scalar", "pe", ph0_done[c - 2])
            sq_done[c] = op("scalar", "activation", SQB[c % 2],
                            H[:, c * T:(c + 1) * T], AF.Square,
                            bias=0.0, scale=1.0, inc="ac")
            wait("tensor", "dv", cast_done[c])
            for n in range(2):
                op("tensor", "matmul", PS[0][0:1, n * NB:(n + 1) * NB], ONESCB,
                   H[:, c * T + n * NB:c * T + n * NB + NB], start=(c == 0),
                   stop=(c == C - 1))
            wait("tensor", "ac", sq_done[c])
            for n in range(2):
                ph0_done[c] = op("tensor", "matmul",
                                 PS[2][0:1, n * NB:(n + 1) * NB], ONESCB,
                                 SQB[c % 2][:, n * NB:(n + 1) * NB],
                                 start=(c == 0), stop=(c == C - 1),
                                 inc="pe" if n == 1 else None)

        def stats_rows(src_s1, src_s2):
            n1 = op("vector", "tensor_scalar", T1, src_s1, 1.0 / D, None,
                    ALU.mult, inc="dv")
            n2 = op("vector", "tensor_scalar", T2, src_s2, 1.0 / D, None,
                    ALU.mult, inc="dv")
            wait("vector", "dv", n1)
            n3 = op("vector", "tensor_tensor", ROW3, T1, T1, ALU.mult,
                    inc="dv")
            wait("vector", "dv", n3)
            dvn = op("vector", "tensor_tensor", T2, T2, ROW3, ALU.subtract,
                     inc="dv")
            wait("scalar", "dv", dvn)
            acn = op("scalar", "activation", ROW3, T2, AF.Sqrt, bias=EPS,
                     scale=1.0, inc="ac")
            wait("vector", "ac", acn)
            n5 = op("vector", "reciprocal", AROW, ROW3, inc="dv")
            wait("vector", "dv", n5)
            return op("vector", "scalar_tensor_tensor", BROW, T1, -1.0, AROW,
                      ALU.mult, ALU.mult, inc="dv")

        def broadcasts_and_copies(rows_n):
            wait("tensor", "dv", rows_n)
            for n in range(2):
                op("tensor", "matmul", PS[1][:, n * NB:(n + 1) * NB], ONESR,
                   AROW[0:1, n * NB:(n + 1) * NB], start=True, stop=True)
            bc = 0
            for n in range(2):
                bc = op("tensor", "matmul", PS[3][:, n * NB:(n + 1) * NB],
                        ONESR, BROW[0:1, n * NB:(n + 1) * NB], start=True,
                        stop=True, inc="pe" if n == 1 else None)
            wait("vector", "pe", bc)
            op("vector", "tensor_copy", ABC, PS[1][:, :])
            return op("vector", "tensor_copy", BBC, PS[3][:, :], inc="dv")

        wait("vector", "pe", ph0_done[C - 1])
        rows1 = stats_rows(PS[0][0:1, :], PS[2][0:1, :])
        abc1 = broadcasts_and_copies(rows1)

        for c0 in (0, 1):
            load_stripe((3 * c0) % 6, w_in[c0])
            load_stripe((3 * c0 + 1) % 6, w_dec[c0])
            load_stripe((3 * c0 + 2) % 6, w_gate[c0])

        h1dv = [0] * C
        h1gp = [0] * C
        wait("vector", "dv", abc1)
        for c in range(C):
            hc = H[:, c * T:(c + 1) * T]
            if c % 3 == 2:
                if c == 2:
                    wait("gpsimd", "dv", abc1)
                else:
                    wait("gpsimd", "gp", CNT["gp"])
                n = op("gpsimd", "tensor_tensor", GPS, hc, ABC, ALU.mult,
                       inc="gp")
                wait("gpsimd", "gp", n)
                h1gp[c] = op("gpsimd", "tensor_tensor", hc, GPS, BBC,
                             ALU.add, inc="gp")
            else:
                n = op("vector", "tensor_tensor", SCRT, hc, ABC, ALU.mult,
                       inc="dv")
                wait("vector", "dv", n)
                h1dv[c] = op("vector", "tensor_tensor", hc, SCRT, BBC,
                             ALU.add, inc="dv")

        def wait_h1(eng, c):
            if c % 3 == 2:
                wait(eng, "gp", h1gp[c])
            else:
                wait(eng, "dv", h1dv[c])

        pe_grp = {}
        act_tile = [0] * C
        dv_tile = [0] * C
        evac_cnt = {}
        gidx = 0
        for c in range(C):
            js = [(3 * c) % 6, (3 * c + 1) % 6, (3 * c + 2) % 6]
            if c >= 2:
                load_stripe(js[0], w_in[c])
                load_stripe(js[1], w_dec[c])
                load_stripe(js[2], w_gate[c])
            for wi, which in enumerate(("inp", "dec", "gate")):
                g = gidx
                gidx += 1
                ps = PS[g % 4]
                wait("tensor", f"ws{js[wi]}", ws_ready[js[wi]])
                if g >= 4:
                    s, v = evac_cnt[g - 4]
                    wait("tensor", s, v)
                for n in range(2):
                    for k in range(C):
                        if c == 0 and wi == 0 and n == 0:
                            wait_h1("tensor", k)
                        pe_grp[(c, which)] = op(
                            "tensor", "matmul", ps[:, n * NB:(n + 1) * NB],
                            WS[js[wi]][:, k * 128:(k + 1) * 128],
                            H[:, k * T + n * NB:k * T + n * NB + NB],
                            start=(k == 0), stop=(k == C - 1),
                            inc="pe" if (n == 1 and k == C - 1) else None)
                ws_free[js[wi]] = ("pe", pe_grp[(c, which)])

            if c >= 1:
                wait("scalar", "dv", dv_tile[c - 1])
            wait("scalar", "pe", pe_grp[(c, "inp")])
            e1 = op("scalar", "activation", SCRB, PS[(3 * c) % 4][:, :],
                    AF.Silu, bias=SM[:, SM_BIN + c:SM_BIN + c + 1], scale=1.0,
                    inc="ac")
            evac_cnt[3 * c] = ("ac", e1)
            wait("scalar", "pe", pe_grp[(c, "dec")])
            e2 = op("scalar", "activation", SCRD, PS[(3 * c + 1) % 4][:, :],
                    AF.Sigmoid, bias=SM2[:, c:c + 1], scale=1.0, inc="ac")
            evac_cnt[3 * c + 1] = ("ac", e2)
            wait("scalar", "pe", pe_grp[(c, "gate")])
            e3 = op("scalar", "activation", yt(c), PS[(3 * c + 2) % 4][:, :],
                    AF.Sigmoid, bias=SM[:, SM_BGATE + c:SM_BGATE + c + 1],
                    scale=1.0, inc="ac")
            evac_cnt[3 * c + 2] = ("ac", e3)
            act_tile[c] = e3

            wait("vector", "ac", e2)
            n = op("vector", "tensor_tensor", SCRT, SCRD, SCRB, ALU.mult,
                   inc="dv")
            wait("vector", "dv", n)
            n = op("vector", "tensor_tensor", SCRB, SCRB, SCRT, ALU.subtract,
                   inc="dv")
            wait("vector", "dv", n)
            op("vector", "tensor_tensor_scan", SCRT, SCRD, SCRB, 0.0,
               ALU.mult, ALU.add, inc="dv")
            n = op("vector", "tensor_tensor_scan", SCRB, SCRD, ZEROS, 1.0,
                   ALU.mult, ALU.add, inc="dv")
            wait("vector", "dv", n)
            op("vector", "tensor_copy", MBOX[:, c:c + 1], SCRT[:, T - 1:T])
            op("vector", "tensor_copy", MBOX[:, 16 + c:17 + c],
               SCRB[:, T - 1:T])
            op("vector", "tensor_copy", z1t(c), SCRT)
            dv_tile[c] = op("vector", "tensor_copy", z2t(c), SCRB, inc="dv")

        wait("sync", "dv", dv_tile[C - 1])
        mb = dma("sync", cc_in.ap(), MBOX, "mb")
        wait("gpsimd", "mb", mb)
        op("gpsimd", "collective_compute", "AllGather", ALU.bypass,
           replica_groups=[[0, 1, 2, 3], [4, 5, 6, 7]],
           ins=[cc_in.ap().opt()], outs=[cc_out.ap().opt()], inc="cc")
        wait("sync", "cc", CNT["cc"])
        mg = dma("sync", MG.rearrange("p (r j) -> p r j", j=32), ccg, "mg")
        wait("vector", "mg", mg)
        op("vector", "memset", CARRY, 0.0, inc="dv")
        n = op("vector", "memset", CT1, 0.0, inc="dv")
        wait("vector", "dv", n)
        for r in range(3):
            n = op("vector", "tensor_tensor", CT2,
                   MG[:, r * 32 + 16:r * 32 + 32], CT1, ALU.mult, inc="dv")
            wait("vector", "dv", n)
            n = op("vector", "tensor_tensor", CT1, MG[:, r * 32:r * 32 + 16],
                   CT2, ALU.add, inc="dv")
            wait("vector", "dv", n)
            n = op("vector", "scalar_tensor_tensor", CARRY, CT1,
                   QS[:, r + 1:r + 2], CARRY, ALU.mult, ALU.add, inc="dv")
            wait("vector", "dv", n)

        corr = [0] * C
        for c in range(C):
            wait("vector", "ac", act_tile[c])
            n = op("vector", "scalar_tensor_tensor", STATES, z2t(c),
                   CARRY[:, c:c + 1], z1t(c), ALU.mult, ALU.add, inc="dv")
            wait("vector", "dv", n)
            corr[c] = op("vector", "tensor_tensor", yt(c), yt(c), STATES,
                         ALU.mult, inc="dv")

        pe_out = [0] * C
        d_evac = [0] * C
        cast2_done = [0] * C
        sq2_done = [0] * C
        e1_done = [0] * C
        for e in range(C):
            jo = (48 + e) % 6
            load_stripe(jo, w_out[e])
            j = e % 2
            if e >= 2:
                wait("sync", "dv", d_evac[e - 2])
            else:
                wait("sync", "dv", cast_done[15 if e == 0 else 13])
            xs_ready[j] = dma("sync", XS[j], xq3[:, e, :], f"xs{j}")

            po = PS[0] if e % 2 == 0 else PS[2]
            wait("tensor", f"ws{jo}", ws_ready[jo])
            if e >= 2:
                wait("tensor", "dv", d_evac[e - 2])
            for n in range(2):
                for c in range(C):
                    if e == 0 and n == 0:
                        wait("tensor", "dv", corr[c])
                    pe_out[e] = op(
                        "tensor", "matmul", po[:, n * NB:(n + 1) * NB],
                        WS[jo][:, c * 128:(c + 1) * 128],
                        yt(c)[:, n * NB:(n + 1) * NB],
                        start=(c == 0), stop=(c == C - 1),
                        inc="pe" if (n == 1 and c == C - 1) else None)
            ws_free[jo] = ("pe", pe_out[e])

            wait("vector", "pe", pe_out[e])
            wait("vector", f"xs{j}", xs_ready[j])
            d_evac[e] = op("vector", "scalar_tensor_tensor",
                           XN[:, e * T:(e + 1) * T], po[:, :],
                           SM[:, SM_BOUT + e:SM_BOUT + e + 1], XS[j],
                           ALU.add, ALU.add, inc="dv")
            wait("vector", "dv", d_evac[e])
            cast2_done[e] = op("vector", "tensor_copy",
                               H[:, e * T:(e + 1) * T],
                               XN[:, e * T:(e + 1) * T], inc="dv")
            wait("scalar", "dv", cast2_done[e])
            if e >= 2:
                wait("scalar", "pe", e1_done[e - 2])
            sq2_done[e] = op("scalar", "activation", SQB[j],
                             H[:, e * T:(e + 1) * T], AF.Square, bias=0.0,
                             scale=1.0, inc="ac")
            wait("tensor", "dv", cast2_done[e])
            for n in range(2):
                op("tensor", "matmul", PS[1][0:1, n * NB:(n + 1) * NB],
                   ONESCB, H[:, e * T + n * NB:e * T + n * NB + NB],
                   start=(e == 0), stop=(e == C - 1))
            wait("tensor", "ac", sq2_done[e])
            for n in range(2):
                e1_done[e] = op("tensor", "matmul",
                                PS[3][0:1, n * NB:(n + 1) * NB], ONESCB,
                                SQB[j][:, n * NB:(n + 1) * NB],
                                start=(e == 0), stop=(e == C - 1),
                                inc="pe" if n == 1 else None)

        wait("vector", "pe", e1_done[C - 1])
        rows2 = stats_rows(PS[1][0:1, :], PS[3][0:1, :])
        abc2 = broadcasts_and_copies(rows2)

        h2dv = [0] * C
        h2gp = [0] * C
        wait("vector", "dv", abc2)
        for c in range(C):
            hc = H[:, c * T:(c + 1) * T]
            xc = XN[:, c * T:(c + 1) * T]
            if c % 3 == 2:
                if c == 2:
                    wait("gpsimd", "dv", abc2)
                n = op("gpsimd", "tensor_tensor", GPS, xc, ABC, ALU.mult)
                h2gp[c] = op("gpsimd", "tensor_tensor", hc, GPS, BBC,
                             ALU.add, inc="gp")
            else:
                n = op("vector", "tensor_tensor", SCRT, xc, ABC, ALU.mult)
                h2dv[c] = op("vector", "tensor_tensor", hc, SCRT, BBC,
                             ALU.add, inc="dv")

        def wait_h2(eng, c):
            if c % 3 == 2:
                wait(eng, "gp", h2gp[c])
            else:
                wait(eng, "dv", h2dv[c])

        wd_free = [("dv", h2dv[15]), ("dv", h2dv[15])]
        wd_ready = [0, 0]
        so_cnt = [0, 0]
        outb_free = [0, 0]
        prev_devac15 = 0
        prev_guevac = None
        for seg in range(NSEG):
            g_done = [0] * SEGH
            u_done = [0] * SEGH
            gevac = [0] * SEGH
            guevac = [0] * SEGH
            for jj in range(SEGH):
                h = seg * SEGH + jj
                jg, ju = (2 * h) % 6, (2 * h + 1) % 6
                load_stripe(jg, wg_t[h])
                load_stripe(ju, wu_t[h])
                pg = PS[0] if jj % 2 == 0 else PS[2]
                pu = PS[1] if jj % 2 == 0 else PS[3]
                wait("tensor", f"ws{jg}", ws_ready[jg])
                wait("tensor", f"ws{ju}", ws_ready[ju])
                if seg == 0 and jj == 0:
                    wait("tensor", "dv", abc2)
                if seg == 1 and jj == 0:
                    wait("tensor", "dv", prev_devac15)
                if jj >= 2:
                    wait("tensor", "ac", gevac[jj - 2])
                    wait("tensor", "dv", guevac[jj - 2])
                for n in range(2):
                    for k in range(C):
                        if seg == 0 and jj == 0 and n == 0:
                            wait_h2("tensor", k)
                        g_done[jj] = op(
                            "tensor", "matmul", pg[:, n * NB:(n + 1) * NB],
                            WS[jg][:, k * 128:(k + 1) * 128],
                            H[:, k * T + n * NB:k * T + n * NB + NB],
                            start=(k == 0), stop=(k == C - 1),
                            inc="pe" if (n == 1 and k == C - 1) else None)
                for n in range(2):
                    for k in range(C):
                        u_done[jj] = op(
                            "tensor", "matmul", pu[:, n * NB:(n + 1) * NB],
                            WS[ju][:, k * 128:(k + 1) * 128],
                            H[:, k * T + n * NB:k * T + n * NB + NB],
                            start=(k == 0), stop=(k == C - 1),
                            inc="pe" if (n == 1 and k == C - 1) else None)
                ws_free[jg] = ("pe", u_done[jj])
                ws_free[ju] = ("pe", u_done[jj])

                wait("scalar", "pe", g_done[jj])
                if jj >= 1:
                    wait("scalar", "dv", guevac[jj - 1])
                elif seg == 1:
                    wait("scalar", "dv", prev_guevac[SEGH - 1])
                gevac[jj] = op("scalar", "activation", GTMP, pg[:, :],
                               AF.Silu, bias=0.0,
                               scale=SM[:, SM_SGATE + h:SM_SGATE + h + 1],
                               inc="ac")

                wait("vector", "pe", u_done[jj])
                wait("vector", "ac", gevac[jj])
                guevac[jj] = op("vector", "scalar_tensor_tensor", gut(jj),
                                pu[:, :], SM[:, SM_SUP + h:SM_SUP + h + 1],
                                GTMP, ALU.mult, ALU.mult, inc="dv")

            wait("tensor", "dv", guevac[SEGH - 1])
            csg = 0
            for jj in range(SEGH):
                for n in range(2):
                    csg = op("tensor", "matmul",
                             PS[1][0:1, n * NB:(n + 1) * NB], ONESCB,
                             gut(jj)[:, n * NB:(n + 1) * NB],
                             start=(jj == 0), stop=(jj == SEGH - 1),
                             inc="pe" if (jj == SEGH - 1 and n == 1) else None)
            wait("vector", "pe", csg)
            crowd_done = op("vector", "tensor_scalar", CROWD, PS[1][0:1, :],
                            -128.0, None, ALU.mult, inc="dv")

            down_done = [0] * C
            devac = [0] * C
            for d in range(C):
                j = d % 2
                s, v = wd_free[j]
                wait("gpsimd", s, v)
                wd_ready[j] = dma("gpsimd", WD[j][:, :], wd_t[seg, d],
                                  f"wd{j}")

                pd = PS[0] if d % 2 == 0 else PS[2]
                wait("tensor", f"wd{j}", wd_ready[j])
                if d == 0:
                    wait("tensor", "dv", crowd_done)
                if d >= 2:
                    wait("tensor", "dv", devac[d - 2])
                for n in range(2):
                    for jj in range(SEGH):
                        op("tensor", "matmul", pd[:, n * NB:(n + 1) * NB],
                           WD[j][:, jj * 128:(jj + 1) * 128],
                           gut(jj)[:, n * NB:(n + 1) * NB],
                           start=(jj == 0), stop=False)
                    down_done[d] = op("tensor", "matmul",
                                      pd[:, n * NB:(n + 1) * NB], ONESRB,
                                      CROWD[0:1, n * NB:(n + 1) * NB],
                                      start=False, stop=True,
                                      inc="pe" if n == 1 else None)
                wd_free[j] = ("pe", down_done[d])

                wait("vector", "pe", down_done[d])
                if seg == 0:
                    devac[d] = op("vector", "scalar_tensor_tensor",
                                  XN[:, d * T:(d + 1) * T], pd[:, :],
                                  SM[:, SM_SDOWN + d:SM_SDOWN + d + 1],
                                  XN[:, d * T:(d + 1) * T], ALU.mult, ALU.add,
                                  inc="dv")
                else:
                    wait("vector", f"so{j}", outb_free[j])
                    devac[d] = op("vector", "scalar_tensor_tensor", OUTB[j],
                                  pd[:, :],
                                  SM[:, SM_SDOWN + d:SM_SDOWN + d + 1],
                                  XN[:, d * T:(d + 1) * T], ALU.mult, ALU.add,
                                  inc="dv")
                    wait("sync", "dv", devac[d])
                    so_cnt[j] = dma("sync", out3[:, d, :], OUTB[j], f"so{j}")
                    outb_free[j] = so_cnt[j]
            prev_guevac = guevac
            prev_devac15 = devac[C - 1]

        wait("sync", "so0", so_cnt[0])
        wait("sync", "so1", so_cnt[1])

        with nc.Block() as block:
            @block.sync
            def _(e):
                for fn in SCHED["sync"]:
                    fn(e)

            @block.gpsimd
            def _(e):
                for fn in SCHED["gpsimd"]:
                    fn(e)

            @block.vector
            def _(e):
                for fn in SCHED["vector"]:
                    fn(e)

            @block.scalar
            def _(e):
                for fn in SCHED["scalar"]:
                    fn(e)

            @block.tensor
            def _(e):
                for fn in SCHED["tensor"]:
                    fn(e)

    return nc


def kernel(**inputs):
    x = np.asarray(inputs["L0_attn_x"], dtype=np.float32)

    def att_w(w):
        a = np.asarray(w, dtype=np.float32).reshape(C, P, C, 128)
        return np.ascontiguousarray(a.transpose(2, 1, 0, 3).reshape(C, P, 2048))

    def ffn_w(w):
        a = np.asarray(w, dtype=np.int32).reshape(HC, 128, C, P)
        return np.ascontiguousarray(a.transpose(0, 3, 2, 1).reshape(HC, P, 2048))

    def down_w(w):
        a = np.asarray(w, dtype=np.int32).reshape(C, 128, NSEG, SEGH, P)
        return np.ascontiguousarray(
            a.transpose(2, 0, 4, 3, 1).reshape(NSEG, C, P, SEGH * 128))

    w_in = att_w(inputs["L0_attn_W_in"])
    w_dec = att_w(inputs["L0_attn_W_decay"])
    w_gate = att_w(inputs["L0_attn_W_gate"])
    w_out = att_w(inputs["L0_attn_W_out"])
    wg_t = ffn_w(inputs["L0_ffn_wq_gate"])
    wu_t = ffn_w(inputs["L0_ffn_wq_up"])
    wd_t = down_w(inputs["L0_ffn_wq_down"])

    def rs(v):
        return np.asarray(v, dtype=np.float32).reshape(-1, 128).T

    smalls = np.ascontiguousarray(np.concatenate(
        [rs(inputs["L0_attn_b_in"]), rs(inputs["L0_attn_b_decay"]),
         rs(inputs["L0_attn_decay_bias"]), rs(inputs["L0_attn_b_gate"]),
         rs(inputs["L0_attn_b_out"]), rs(inputs["L0_ffn_s_down"]),
         rs(inputs["L0_ffn_s_gate"]), rs(inputs["L0_ffn_s_up"])], axis=1))

    xT = np.ascontiguousarray(np.transpose(x, (0, 2, 1)))

    nc = build_program()
    in_maps = []
    for core in range(8):
        b, q = core // 4, core % 4
        qs = np.zeros((P, 4), dtype=np.float32)
        qs[:, q] = 1.0
        in_maps.append(dict(
            xq=np.ascontiguousarray(xT[b][:, q * T:(q + 1) * T]),
            w_in=w_in, w_dec=w_dec, w_gate=w_gate, w_out=w_out,
            wg_t=wg_t, wu_t=wu_t, wd_t=wd_t, smalls=smalls, qsel=qs))

    res = run_bass_kernel_spmd(nc, in_maps, core_ids=list(range(8)))

    outT = np.empty((2, D, 4096), dtype=np.float32)
    for core in range(8):
        b, q = core // 4, core % 4
        outT[b][:, q * T:(q + 1) * T] = np.asarray(res.results[core]["outT"])
    return np.ascontiguousarray(np.transpose(outT, (0, 2, 1)))


# version 06
# speedup vs baseline: 1.0188x; 1.0188x over previous
import numpy as np
import concourse.bass as bass
import concourse.mybir as mybir
from concourse.bass_utils import run_bass_kernel_spmd
from contextlib import ExitStack

F32 = mybir.dt.float32
BF16 = mybir.dt.bfloat16
I32 = mybir.dt.int32
AF = mybir.ActivationFunctionType
ALU = mybir.AluOpType

P = 128
T = 1024
C = 16
D = 2048
HID = 5632
HC = 44
SEGH = 22
NSEG = 2
NB = 512
EPSV = 1e-5

OFF_SM = 0
OFF_SM2 = 736
OFF_QS = 800
OFF_MBOX = 816
OFF_MG = 944
OFF_CARRY = 1456
OFF_CT1 = 1520
OFF_CT2 = 1584
OFF_ONESR = 1648
OFF_ONESRB = 2160
OFF_ONESC = 2416
OFF_ONESCB = 2420
OFF_EPS = 2424
OFF_CROWD = 4480
OFF_H = 6656
OFF_BIG = 39424
OFF_WS = 137728
OFF_A1 = 162304
OFF_A2 = 166400
OFF_A3 = 170496
OFF_A4 = 174592
OFF_A5 = 178688
OFF_A6 = 182784
OFF_A7 = 186880
TOTAL_BYTES = 207360

SM_BIN = 0
SM_BDEC = 16
SM_DBIAS = 32
SM_BGATE = 48
SM_BOUT = 64
SM_SDOWN = 80
SM_SGATE = 96
SM_SUP = 140


def build_program():
    nc = bass.Bass("TRN2", target_bir_lowering=False)

    xq = nc.dram_tensor("xq", [D, T], F32, kind="ExternalInput").ap()
    w_in = nc.dram_tensor("w_in", [C, P, 2048], F32, kind="ExternalInput").ap()
    w_dec = nc.dram_tensor("w_dec", [C, P, 2048], F32, kind="ExternalInput").ap()
    w_gate = nc.dram_tensor("w_gate", [C, P, 2048], F32, kind="ExternalInput").ap()
    w_out = nc.dram_tensor("w_out", [C, P, 2048], F32, kind="ExternalInput").ap()
    wg_t = nc.dram_tensor("wg_t", [HC, P, 2048], I32, kind="ExternalInput").ap()
    wu_t = nc.dram_tensor("wu_t", [HC, P, 2048], I32, kind="ExternalInput").ap()
    wd_t = nc.dram_tensor("wd_t", [NSEG, C, P, SEGH * 128], I32,
                          kind="ExternalInput").ap()
    smalls = nc.dram_tensor("smalls", [P, 184], F32, kind="ExternalInput").ap()
    qsel = nc.dram_tensor("qsel", [P, 4], F32, kind="ExternalInput").ap()
    outT = nc.dram_tensor("outT", [D, T], F32, kind="ExternalOutput").ap()
    cc_in = nc.dram_tensor("cc_in", [P, 32], F32)
    cc_out = nc.dram_tensor("cc_out", [4 * P, 32], F32)

    xq3 = xq.rearrange("(c p) t -> p c t", p=P)
    out3 = outT.rearrange("(c p) t -> p c t", p=P)
    ccg = cc_out.rearrange("(r p) j -> p r j", p=P)

    with ExitStack() as ctx:
        AR = ctx.enter_context(nc.sbuf_tensor("AR", [P, TOTAL_BYTES // 2], BF16))

        def cv_f32(off_b, rows, cols):
            v = AR[:, off_b // 2:(off_b + cols * 4) // 2].bitcast(F32)
            return v[0:rows, :] if rows != P else v

        def cv_bf(off_b, rows, cols):
            v = AR[:, off_b // 2:(off_b + cols * 2) // 2]
            return v[0:rows, :] if rows != P else v

        SM = cv_f32(OFF_SM, P, 184)
        SM2 = cv_f32(OFF_SM2, P, 16)
        QS = cv_f32(OFF_QS, P, 4)
        MBOX = cv_f32(OFF_MBOX, P, 32)
        MG = cv_f32(OFF_MG, P, 128)
        CARRY = cv_f32(OFF_CARRY, P, 16)
        CT1 = cv_f32(OFF_CT1, P, 16)
        CT2 = cv_f32(OFF_CT2, P, 16)
        ONESR = cv_f32(OFF_ONESR, 1, 128)
        ONESRB = cv_bf(OFF_ONESRB, 1, 128)
        ONESCB = cv_bf(OFF_ONESCB, P, 1)
        EPS = cv_f32(OFF_EPS, 1, 1)
        CROWD = cv_bf(OFF_CROWD, 1, 1024)
        H = cv_bf(OFF_H, P, C * T)
        XN = cv_f32(OFF_BIG, P, C * T)

        def z1t(c):
            return cv_bf(OFF_BIG + c * 2048, P, T)

        def z2t(c):
            return cv_bf(OFF_BIG + 32768 + c * 2048, P, T)

        def yt(c):
            return cv_bf(OFF_BIG + 65536 + c * 2048, P, T)

        def gut(j):
            if j < 16:
                return yt(j)
            return cv_bf(OFF_A7 + (j - 16) * 2048, P, T)

        WS = [cv_bf(OFF_WS + j * 4096, P, 2048) for j in range(6)]
        WD = [cv_bf(OFF_A3, P, SEGH * 128), cv_bf(OFF_A5, P, SEGH * 128)]
        XS = [cv_f32(OFF_A1, P, T), cv_f32(OFF_A2, P, T), cv_f32(OFF_A4, P, T)]
        OUTB = XS
        SQB = [cv_bf(OFF_A3, P, T), cv_bf(OFF_A3 + 2048, P, T)]
        ABC = cv_f32(OFF_A3, P, T)
        BBC = cv_f32(OFF_A4, P, T)
        SCRD = cv_f32(OFF_A3, P, T)
        SCRB = cv_f32(OFF_A4, P, T)
        SCRT = cv_f32(OFF_A5, P, T)
        ZEROS = cv_f32(OFF_A6, P, T)
        STATES = cv_f32(OFF_A5, P, T)
        GPS = cv_f32(OFF_A7, P, T)
        T1 = cv_f32(OFF_A7, 1, 1024)
        T2 = cv_f32(OFF_A7 + 4096, 1, 1024)
        ROW3 = cv_f32(OFF_A7 + 8192, 1, 1024)
        AROW = cv_f32(OFF_A7 + 12288, 1, 1024)
        BROW = cv_f32(OFF_A7 + 16384, 1, 1024)
        GTMP = cv_f32(OFF_A7 + 12288, P, T)

        PS = [ctx.enter_context(nc.psum_tensor(f"PS_{i}", [P, 1024], F32))
              for i in range(4)]

        sem_names = (["ld", "mb", "cc", "mg", "pe", "dv", "ac", "gp",
                      "xs0", "xs1", "xs2", "wd0", "wd1", "so0", "so1"]
                     + [f"ws{j}" for j in range(6)])
        SEMS = {n: ctx.enter_context(nc.semaphore(n)) for n in sem_names}
        CNT = {n: 0 for n in sem_names}

        SCHED = {"sync": [], "gpsimd": [], "vector": [], "scalar": [],
                 "tensor": []}

        def emit(eng, fn):
            SCHED[eng].append(fn)

        def wait(eng, sem, val):
            if val <= 0:
                return
            emit(eng, lambda e, h=SEMS[sem], v=val: e.wait_ge(h, v))

        def op(eng, meth, *args, inc=None, incby=1, **kw):
            if eng in ("vector", "gpsimd") and meth != "dma_start":
                esem = "dv" if eng == "vector" else "gp"
                wait(eng, esem, CNT[esem])
                if inc is None:
                    inc, incby = esem, 1
            if inc:
                CNT[inc] += incby
            h = SEMS[inc] if inc else None
            def fn(e, meth=meth, args=args, kw=kw, h=h, incby=incby):
                r = getattr(e, meth)(*args, **kw)
                if h is not None:
                    r.then_inc(h, incby)
            emit(eng, fn)
            return CNT[inc] if inc else None

        def dma(queue, out, in_, sem):
            return op(queue, "dma_start", inc=sem, incby=16, out=out, in_=in_)

        ws_free = [("pe", 0)] * 6
        ws_ready = [0] * 6

        def load_stripe(j, src_ap):
            s, v = ws_free[j]
            wait("gpsimd", s, v)
            ws_ready[j] = dma("gpsimd", WS[j][:, :], src_ap, f"ws{j}")

        dma("sync", SM, smalls, "ld")
        dma("sync", QS, qsel, "ld")
        op("vector", "memset", ONESR, 1.0)
        op("vector", "memset", ONESRB, 1.0)
        op("vector", "memset", ONESCB, 1.0)
        op("vector", "memset", EPS, EPSV)
        op("vector", "memset", ZEROS, 0.0)
        wait("vector", "ld", 32)
        op("vector", "tensor_tensor", SM2, SM[:, 16:32], SM[:, 32:48], ALU.add)

        ph0_done = [0] * C
        cast_done = [0] * C
        sq_done = [0] * C
        xs_ready = [0, 0, 0]
        for c in range(C):
            j = c % 3
            if c >= 3:
                wait("sync", "dv", cast_done[c - 3])
            xs_ready[j] = dma("sync", XS[j], xq3[:, c, :], f"xs{j}")
            wait("vector", f"xs{j}", xs_ready[j])
            cast_done[c] = op("vector", "tensor_copy", H[:, c * T:(c + 1) * T],
                              XS[j], inc="dv")
            wait("scalar", "dv", cast_done[c])
            if c >= 2:
                wait("scalar", "pe", ph0_done[c - 2])
            sq_done[c] = op("scalar", "activation", SQB[c % 2],
                            H[:, c * T:(c + 1) * T], AF.Square,
                            bias=0.0, scale=1.0, inc="ac")
            wait("tensor", "dv", cast_done[c])
            for n in range(2):
                op("tensor", "matmul", PS[0][0:1, n * NB:(n + 1) * NB], ONESCB,
                   H[:, c * T + n * NB:c * T + n * NB + NB], start=(c == 0),
                   stop=(c == C - 1))
            wait("tensor", "ac", sq_done[c])
            for n in range(2):
                ph0_done[c] = op("tensor", "matmul",
                                 PS[2][0:1, n * NB:(n + 1) * NB], ONESCB,
                                 SQB[c % 2][:, n * NB:(n + 1) * NB],
                                 start=(c == 0), stop=(c == C - 1),
                                 inc="pe" if n == 1 else None)

        def stats_rows(src_s1, src_s2):
            n1 = op("vector", "tensor_scalar", T1, src_s1, 1.0 / D, None,
                    ALU.mult, inc="dv")
            n2 = op("vector", "tensor_scalar", T2, src_s2, 1.0 / D, None,
                    ALU.mult, inc="dv")
            wait("vector", "dv", n1)
            n3 = op("vector", "tensor_tensor", ROW3, T1, T1, ALU.mult,
                    inc="dv")
            wait("vector", "dv", n3)
            dvn = op("vector", "tensor_tensor", T2, T2, ROW3, ALU.subtract,
                     inc="dv")
            wait("scalar", "dv", dvn)
            acn = op("scalar", "activation", ROW3, T2, AF.Sqrt, bias=EPS,
                     scale=1.0, inc="ac")
            wait("vector", "ac", acn)
            n5 = op("vector", "reciprocal", AROW, ROW3, inc="dv")
            wait("vector", "dv", n5)
            return op("vector", "scalar_tensor_tensor", BROW, T1, -1.0, AROW,
                      ALU.mult, ALU.mult, inc="dv")

        def broadcasts_and_copies(rows_n):
            wait("tensor", "dv", rows_n)
            for n in range(2):
                op("tensor", "matmul", PS[1][:, n * NB:(n + 1) * NB], ONESR,
                   AROW[0:1, n * NB:(n + 1) * NB], start=True, stop=True)
            bc = 0
            for n in range(2):
                bc = op("tensor", "matmul", PS[3][:, n * NB:(n + 1) * NB],
                        ONESR, BROW[0:1, n * NB:(n + 1) * NB], start=True,
                        stop=True, inc="pe" if n == 1 else None)
            wait("vector", "pe", bc)
            op("vector", "tensor_copy", ABC, PS[1][:, :])
            return op("vector", "tensor_copy", BBC, PS[3][:, :], inc="dv")

        wait("vector", "pe", ph0_done[C - 1])
        rows1 = stats_rows(PS[0][0:1, :], PS[2][0:1, :])
        abc1 = broadcasts_and_copies(rows1)

        for c0 in (0, 1):
            load_stripe((3 * c0) % 6, w_in[c0])
            load_stripe((3 * c0 + 1) % 6, w_dec[c0])
            load_stripe((3 * c0 + 2) % 6, w_gate[c0])

        h1dv = [0] * C
        h1gp = [0] * C
        wait("vector", "dv", abc1)
        for c in range(C):
            hc = H[:, c * T:(c + 1) * T]
            if c % 3 == 2:
                if c == 2:
                    wait("gpsimd", "dv", abc1)
                else:
                    wait("gpsimd", "gp", CNT["gp"])
                n = op("gpsimd", "tensor_tensor", GPS, hc, ABC, ALU.mult,
                       inc="gp")
                wait("gpsimd", "gp", n)
                h1gp[c] = op("gpsimd", "tensor_tensor", hc, GPS, BBC,
                             ALU.add, inc="gp")
            else:
                n = op("vector", "tensor_tensor", SCRT, hc, ABC, ALU.mult,
                       inc="dv")
                wait("vector", "dv", n)
                h1dv[c] = op("vector", "tensor_tensor", hc, SCRT, BBC,
                             ALU.add, inc="dv")

        def wait_h1(eng, c):
            if c % 3 == 2:
                wait(eng, "gp", h1gp[c])
            else:
                wait(eng, "dv", h1dv[c])

        pe_grp = {}
        act_tile = [0] * C
        dv_tile = [0] * C
        evac_cnt = {}
        gidx = 0
        for c in range(C):
            js = [(3 * c) % 6, (3 * c + 1) % 6, (3 * c + 2) % 6]
            if c >= 2:
                load_stripe(js[0], w_in[c])
                load_stripe(js[1], w_dec[c])
                load_stripe(js[2], w_gate[c])
            for wi, which in enumerate(("inp", "dec", "gate")):
                g = gidx
                gidx += 1
                ps = PS[g % 4]
                wait("tensor", f"ws{js[wi]}", ws_ready[js[wi]])
                if g >= 4:
                    s, v = evac_cnt[g - 4]
                    wait("tensor", s, v)
                for n in range(2):
                    for k in range(C):
                        if c == 0 and wi == 0 and n == 0:
                            wait_h1("tensor", k)
                        pe_grp[(c, which)] = op(
                            "tensor", "matmul", ps[:, n * NB:(n + 1) * NB],
                            WS[js[wi]][:, k * 128:(k + 1) * 128],
                            H[:, k * T + n * NB:k * T + n * NB + NB],
                            start=(k == 0), stop=(k == C - 1),
                            inc="pe" if (n == 1 and k == C - 1) else None)
                ws_free[js[wi]] = ("pe", pe_grp[(c, which)])

            if c >= 1:
                wait("scalar", "dv", dv_tile[c - 1])
            wait("scalar", "pe", pe_grp[(c, "inp")])
            e1 = op("scalar", "activation", SCRB, PS[(3 * c) % 4][:, :],
                    AF.Silu, bias=SM[:, SM_BIN + c:SM_BIN + c + 1], scale=1.0,
                    inc="ac")
            evac_cnt[3 * c] = ("ac", e1)
            wait("scalar", "pe", pe_grp[(c, "dec")])
            e2 = op("scalar", "activation", SCRD, PS[(3 * c + 1) % 4][:, :],
                    AF.Sigmoid, bias=SM2[:, c:c + 1], scale=1.0, inc="ac")
            evac_cnt[3 * c + 1] = ("ac", e2)
            wait("scalar", "pe", pe_grp[(c, "gate")])
            e3 = op("scalar", "activation", yt(c), PS[(3 * c + 2) % 4][:, :],
                    AF.Sigmoid, bias=SM[:, SM_BGATE + c:SM_BGATE + c + 1],
                    scale=1.0, inc="ac")
            evac_cnt[3 * c + 2] = ("ac", e3)
            act_tile[c] = e3

            wait("vector", "ac", e2)
            n = op("vector", "tensor_tensor", SCRT, SCRD, SCRB, ALU.mult,
                   inc="dv")
            wait("vector", "dv", n)
            n = op("vector", "tensor_tensor", SCRB, SCRB, SCRT, ALU.subtract,
                   inc="dv")
            wait("vector", "dv", n)
            op("vector", "tensor_tensor_scan", SCRT, SCRD, SCRB, 0.0,
               ALU.mult, ALU.add, inc="dv")
            op("vector", "tensor_tensor_scan", SCRB, SCRD, ZEROS, 1.0,
               ALU.mult, ALU.add, inc="dv")
            op("vector", "tensor_copy", MBOX[:, c:c + 1], SCRT[:, T - 1:T])
            op("vector", "tensor_copy", MBOX[:, 16 + c:17 + c],
               SCRB[:, T - 1:T])
            wait("vector", "ac", act_tile[c])
            op("vector", "tensor_tensor", z1t(c), yt(c), SCRT, ALU.mult)
            dv_tile[c] = op("vector", "tensor_tensor", z2t(c), yt(c), SCRB,
                            ALU.mult, inc="dv")

        for e0 in (0, 1):
            load_stripe((48 + e0) % 6, w_out[e0])
        wait("sync", "dv", dv_tile[C - 1])
        mb = dma("sync", cc_in.ap(), MBOX, "mb")
        wait("gpsimd", "mb", mb)
        op("gpsimd", "collective_compute", "AllGather", ALU.bypass,
           replica_groups=[[0, 1, 2, 3], [4, 5, 6, 7]],
           ins=[cc_in.ap().opt()], outs=[cc_out.ap().opt()], inc="cc")
        wait("sync", "cc", CNT["cc"])
        mg = dma("sync", MG.rearrange("p (r j) -> p r j", j=32), ccg, "mg")
        wait("vector", "mg", mg)
        op("vector", "memset", CARRY, 0.0, inc="dv")
        n = op("vector", "memset", CT1, 0.0, inc="dv")
        wait("vector", "dv", n)
        for r in range(3):
            n = op("vector", "tensor_tensor", CT2,
                   MG[:, r * 32 + 16:r * 32 + 32], CT1, ALU.mult, inc="dv")
            wait("vector", "dv", n)
            n = op("vector", "tensor_tensor", CT1, MG[:, r * 32:r * 32 + 16],
                   CT2, ALU.add, inc="dv")
            wait("vector", "dv", n)
            n = op("vector", "scalar_tensor_tensor", CARRY, CT1,
                   QS[:, r + 1:r + 2], CARRY, ALU.mult, ALU.add, inc="dv")
            wait("vector", "dv", n)

        corr = [None] * C
        for c in range(C):
            n = op("vector", "scalar_tensor_tensor", yt(c), z2t(c),
                   CARRY[:, c:c + 1], z1t(c), ALU.mult, ALU.add)
            corr[c] = ("dv", n)

        pe_out = [0] * C
        d_evac = [0] * C
        cast2_done = [0] * C
        sq2_done = [0] * C
        e1_done = [0] * C
        for e in range(C):
            jo = (48 + e) % 6
            if e >= 2:
                load_stripe(jo, w_out[e])
            j = e % 2
            if e >= 2:
                wait("sync", "dv", d_evac[e - 2])
            else:
                wait("sync", "dv", cast_done[15 if e == 0 else 13])
            xs_ready[j] = dma("sync", XS[j], xq3[:, e, :], f"xs{j}")

            po = PS[0] if e % 2 == 0 else PS[2]
            wait("tensor", f"ws{jo}", ws_ready[jo])
            if e >= 2:
                wait("tensor", "dv", d_evac[e - 2])
            for n in range(2):
                for c in range(C):
                    if e == 0 and n == 0:
                        wait("tensor", corr[c][0], corr[c][1])
                    pe_out[e] = op(
                        "tensor", "matmul", po[:, n * NB:(n + 1) * NB],
                        WS[jo][:, c * 128:(c + 1) * 128],
                        yt(c)[:, n * NB:(n + 1) * NB],
                        start=(c == 0), stop=(c == C - 1),
                        inc="pe" if (n == 1 and c == C - 1) else None)
            ws_free[jo] = ("pe", pe_out[e])

            wait("vector", "pe", pe_out[e])
            wait("vector", f"xs{j}", xs_ready[j])
            d_evac[e] = op("vector", "scalar_tensor_tensor",
                           XN[:, e * T:(e + 1) * T], po[:, :],
                           SM[:, SM_BOUT + e:SM_BOUT + e + 1], XS[j],
                           ALU.add, ALU.add, inc="dv")
            wait("vector", "dv", d_evac[e])
            cast2_done[e] = op("vector", "tensor_copy",
                               H[:, e * T:(e + 1) * T],
                               XN[:, e * T:(e + 1) * T], inc="dv")
            wait("scalar", "dv", cast2_done[e])
            if e >= 2:
                wait("scalar", "pe", e1_done[e - 2])
            sq2_done[e] = op("scalar", "activation", SQB[j],
                             H[:, e * T:(e + 1) * T], AF.Square, bias=0.0,
                             scale=1.0, inc="ac")
            wait("tensor", "dv", cast2_done[e])
            for n in range(2):
                op("tensor", "matmul", PS[1][0:1, n * NB:(n + 1) * NB],
                   ONESCB, H[:, e * T + n * NB:e * T + n * NB + NB],
                   start=(e == 0), stop=(e == C - 1))
            wait("tensor", "ac", sq2_done[e])
            for n in range(2):
                e1_done[e] = op("tensor", "matmul",
                                PS[3][0:1, n * NB:(n + 1) * NB], ONESCB,
                                SQB[j][:, n * NB:(n + 1) * NB],
                                start=(e == 0), stop=(e == C - 1),
                                inc="pe" if n == 1 else None)

        wait("vector", "pe", e1_done[C - 1])
        rows2 = stats_rows(PS[1][0:1, :], PS[3][0:1, :])
        abc2 = broadcasts_and_copies(rows2)

        h2dv = [0] * C
        h2gp = [0] * C
        wait("vector", "dv", abc2)
        for c in range(C):
            hc = H[:, c * T:(c + 1) * T]
            xc = XN[:, c * T:(c + 1) * T]
            if c % 3 == 2:
                if c == 2:
                    wait("gpsimd", "dv", abc2)
                n = op("gpsimd", "tensor_tensor", GPS, xc, ABC, ALU.mult)
                h2gp[c] = op("gpsimd", "tensor_tensor", hc, GPS, BBC,
                             ALU.add, inc="gp")
            else:
                n = op("vector", "tensor_tensor", SCRT, xc, ABC, ALU.mult)
                h2dv[c] = op("vector", "tensor_tensor", hc, SCRT, BBC,
                             ALU.add, inc="dv")

        def wait_h2(eng, c):
            if c % 3 == 2:
                wait(eng, "gp", h2gp[c])
            else:
                wait(eng, "dv", h2dv[c])

        wd_free = [("dv", h2dv[15]), ("dv", h2dv[15])]
        wd_ready = [0, 0]
        so_cnt = [0, 0]
        outb_free = [0, 0]
        prev_devac15 = 0
        prev_guevac = None
        for seg in range(NSEG):
            g_done = [0] * SEGH
            u_done = [0] * SEGH
            gevac = [0] * SEGH
            guevac = [0] * SEGH
            for jj in range(SEGH):
                h = seg * SEGH + jj
                jg, ju = (2 * h) % 6, (2 * h + 1) % 6
                load_stripe(jg, wg_t[h])
                load_stripe(ju, wu_t[h])
                pg = PS[0] if jj % 2 == 0 else PS[2]
                pu = PS[1] if jj % 2 == 0 else PS[3]
                wait("tensor", f"ws{jg}", ws_ready[jg])
                wait("tensor", f"ws{ju}", ws_ready[ju])
                if seg == 0 and jj == 0:
                    wait("tensor", "dv", abc2)
                if seg == 1 and jj == 0:
                    wait("tensor", "dv", prev_devac15)
                if jj >= 2:
                    wait("tensor", "ac", gevac[jj - 2])
                    wait("tensor", "dv", guevac[jj - 2])
                for n in range(2):
                    for k in range(C):
                        if seg == 0 and jj == 0 and n == 0:
                            wait_h2("tensor", k)
                        g_done[jj] = op(
                            "tensor", "matmul", pg[:, n * NB:(n + 1) * NB],
                            WS[jg][:, k * 128:(k + 1) * 128],
                            H[:, k * T + n * NB:k * T + n * NB + NB],
                            start=(k == 0), stop=(k == C - 1),
                            inc="pe" if (n == 1 and k == C - 1) else None)
                for n in range(2):
                    for k in range(C):
                        u_done[jj] = op(
                            "tensor", "matmul", pu[:, n * NB:(n + 1) * NB],
                            WS[ju][:, k * 128:(k + 1) * 128],
                            H[:, k * T + n * NB:k * T + n * NB + NB],
                            start=(k == 0), stop=(k == C - 1),
                            inc="pe" if (n == 1 and k == C - 1) else None)
                ws_free[jg] = ("pe", u_done[jj])
                ws_free[ju] = ("pe", u_done[jj])

                wait("scalar", "pe", g_done[jj])
                if jj >= 1:
                    wait("scalar", "dv", guevac[jj - 1])
                elif seg == 1:
                    wait("scalar", "dv", prev_guevac[SEGH - 1])
                gevac[jj] = op("scalar", "activation", GTMP, pg[:, :],
                               AF.Silu, bias=0.0,
                               scale=SM[:, SM_SGATE + h:SM_SGATE + h + 1],
                               inc="ac")

                wait("vector", "pe", u_done[jj])
                wait("vector", "ac", gevac[jj])
                guevac[jj] = op("vector", "scalar_tensor_tensor", gut(jj),
                                pu[:, :], SM[:, SM_SUP + h:SM_SUP + h + 1],
                                GTMP, ALU.mult, ALU.mult, inc="dv")

            wait("tensor", "dv", guevac[SEGH - 1])
            csg = 0
            for jj in range(SEGH):
                for n in range(2):
                    csg = op("tensor", "matmul",
                             PS[1][0:1, n * NB:(n + 1) * NB], ONESCB,
                             gut(jj)[:, n * NB:(n + 1) * NB],
                             start=(jj == 0), stop=(jj == SEGH - 1),
                             inc="pe" if (jj == SEGH - 1 and n == 1) else None)
            wait("vector", "pe", csg)
            crowd_done = op("vector", "tensor_scalar", CROWD, PS[1][0:1, :],
                            -128.0, None, ALU.mult, inc="dv")

            down_done = [0] * C
            devac = [0] * C
            for d in range(C):
                j = d % 2
                s, v = wd_free[j]
                wait("gpsimd", s, v)
                wd_ready[j] = dma("gpsimd", WD[j][:, :], wd_t[seg, d],
                                  f"wd{j}")

                pd = PS[0] if d % 2 == 0 else PS[2]
                wait("tensor", f"wd{j}", wd_ready[j])
                if d == 0:
                    wait("tensor", "dv", crowd_done)
                if d >= 2:
                    wait("tensor", "dv", devac[d - 2])
                for n in range(2):
                    for jj in range(SEGH):
                        op("tensor", "matmul", pd[:, n * NB:(n + 1) * NB],
                           WD[j][:, jj * 128:(jj + 1) * 128],
                           gut(jj)[:, n * NB:(n + 1) * NB],
                           start=(jj == 0), stop=False)
                    down_done[d] = op("tensor", "matmul",
                                      pd[:, n * NB:(n + 1) * NB], ONESRB,
                                      CROWD[0:1, n * NB:(n + 1) * NB],
                                      start=False, stop=True,
                                      inc="pe" if n == 1 else None)
                wd_free[j] = ("pe", down_done[d])

                wait("vector", "pe", down_done[d])
                if seg == 0:
                    devac[d] = op("vector", "scalar_tensor_tensor",
                                  XN[:, d * T:(d + 1) * T], pd[:, :],
                                  SM[:, SM_SDOWN + d:SM_SDOWN + d + 1],
                                  XN[:, d * T:(d + 1) * T], ALU.mult, ALU.add,
                                  inc="dv")
                else:
                    wait("vector", f"so{j}", outb_free[j])
                    devac[d] = op("vector", "scalar_tensor_tensor", OUTB[j],
                                  pd[:, :],
                                  SM[:, SM_SDOWN + d:SM_SDOWN + d + 1],
                                  XN[:, d * T:(d + 1) * T], ALU.mult, ALU.add,
                                  inc="dv")
                    wait("sync", "dv", devac[d])
                    so_cnt[j] = dma("sync", out3[:, d, :], OUTB[j], f"so{j}")
                    outb_free[j] = so_cnt[j]
            prev_guevac = guevac
            prev_devac15 = devac[C - 1]

        wait("sync", "so0", so_cnt[0])
        wait("sync", "so1", so_cnt[1])

        with nc.Block() as block:
            @block.sync
            def _(e):
                for fn in SCHED["sync"]:
                    fn(e)

            @block.gpsimd
            def _(e):
                for fn in SCHED["gpsimd"]:
                    fn(e)

            @block.vector
            def _(e):
                for fn in SCHED["vector"]:
                    fn(e)

            @block.scalar
            def _(e):
                for fn in SCHED["scalar"]:
                    fn(e)

            @block.tensor
            def _(e):
                for fn in SCHED["tensor"]:
                    fn(e)

    return nc


def kernel(**inputs):
    x = np.asarray(inputs["L0_attn_x"], dtype=np.float32)

    def att_w(w):
        a = np.asarray(w, dtype=np.float32).reshape(C, P, C, 128)
        return np.ascontiguousarray(a.transpose(2, 1, 0, 3).reshape(C, P, 2048))

    def ffn_w(w):
        a = np.asarray(w, dtype=np.int32).reshape(HC, 128, C, P)
        return np.ascontiguousarray(a.transpose(0, 3, 2, 1).reshape(HC, P, 2048))

    def down_w(w):
        a = np.asarray(w, dtype=np.int32).reshape(C, 128, NSEG, SEGH, P)
        return np.ascontiguousarray(
            a.transpose(2, 0, 4, 3, 1).reshape(NSEG, C, P, SEGH * 128))

    w_in = att_w(inputs["L0_attn_W_in"])
    w_dec = att_w(inputs["L0_attn_W_decay"])
    w_gate = att_w(inputs["L0_attn_W_gate"])
    w_out = att_w(inputs["L0_attn_W_out"])
    wg_t = ffn_w(inputs["L0_ffn_wq_gate"])
    wu_t = ffn_w(inputs["L0_ffn_wq_up"])
    wd_t = down_w(inputs["L0_ffn_wq_down"])

    def rs(v):
        return np.asarray(v, dtype=np.float32).reshape(-1, 128).T

    smalls = np.ascontiguousarray(np.concatenate(
        [rs(inputs["L0_attn_b_in"]), rs(inputs["L0_attn_b_decay"]),
         rs(inputs["L0_attn_decay_bias"]), rs(inputs["L0_attn_b_gate"]),
         rs(inputs["L0_attn_b_out"]), rs(inputs["L0_ffn_s_down"]),
         rs(inputs["L0_ffn_s_gate"]), rs(inputs["L0_ffn_s_up"])], axis=1))

    xT = np.ascontiguousarray(np.transpose(x, (0, 2, 1)))

    nc = build_program()
    in_maps = []
    for core in range(8):
        b, q = core // 4, core % 4
        qs = np.zeros((P, 4), dtype=np.float32)
        qs[:, q] = 1.0
        in_maps.append(dict(
            xq=np.ascontiguousarray(xT[b][:, q * T:(q + 1) * T]),
            w_in=w_in, w_dec=w_dec, w_gate=w_gate, w_out=w_out,
            wg_t=wg_t, wu_t=wu_t, wd_t=wd_t, smalls=smalls, qsel=qs))

    res = run_bass_kernel_spmd(nc, in_maps, core_ids=list(range(8)))

    outT = np.empty((2, D, 4096), dtype=np.float32)
    for core in range(8):
        b, q = core // 4, core % 4
        outT[b][:, q * T:(q + 1) * T] = np.asarray(res.results[core]["outT"])
    return np.ascontiguousarray(np.transpose(outT, (0, 2, 1)))


# version 07
# speedup vs baseline: 1.0929x; 1.0727x over previous
import numpy as np
import concourse.bass as bass
import concourse.mybir as mybir
from concourse.bass_utils import run_bass_kernel_spmd
from contextlib import ExitStack

F32 = mybir.dt.float32
BF16 = mybir.dt.bfloat16
I32 = mybir.dt.int32
AF = mybir.ActivationFunctionType
ALU = mybir.AluOpType

P = 128
T = 1024
C = 16
D = 2048
HID = 5632
HC = 44
SEGH = 22
NSEG = 2
NB = 512
EPSV = 1e-5

OFF_SM = 0
OFF_SM2 = 736
OFF_QS = 800
OFF_MBOX = 816
OFF_MG = 944
OFF_CARRY = 1456
OFF_CT1 = 1520
OFF_CT2 = 1584
OFF_ONESR = 1648
OFF_ONESRB = 2160
OFF_ONESC = 2416
OFF_ONESCB = 2420
OFF_EPS = 2424
OFF_CROWD = 4480
OFF_H = 6656
OFF_BIG = 39424
OFF_WS = 137728
OFF_A1 = 162304
OFF_A2 = 166400
OFF_A3 = 170496
OFF_A4 = 174592
OFF_A5 = 178688
OFF_A6 = 182784
OFF_A7 = 186880
TOTAL_BYTES = 207360

SM_BIN = 0
SM_BDEC = 16
SM_DBIAS = 32
SM_BGATE = 48
SM_BOUT = 64
SM_SDOWN = 80
SM_SGATE = 96
SM_SUP = 140


def build_program():
    nc = bass.Bass("TRN2", target_bir_lowering=False)

    xq = nc.dram_tensor("xq", [D, T], F32, kind="ExternalInput").ap()
    w_in = nc.dram_tensor("w_in", [C, P, 2048], F32, kind="ExternalInput").ap()
    w_dec = nc.dram_tensor("w_dec", [C, P, 2048], F32, kind="ExternalInput").ap()
    w_gate = nc.dram_tensor("w_gate", [C, P, 2048], F32, kind="ExternalInput").ap()
    w_out = nc.dram_tensor("w_out", [C, P, 2048], F32, kind="ExternalInput").ap()
    wg_t = nc.dram_tensor("wg_t", [HC, P, 2048], I32, kind="ExternalInput").ap()
    wu_t = nc.dram_tensor("wu_t", [HC, P, 2048], I32, kind="ExternalInput").ap()
    wd_t = nc.dram_tensor("wd_t", [NSEG, C, P, SEGH * 128], I32,
                          kind="ExternalInput").ap()
    smalls = nc.dram_tensor("smalls", [P, 184], F32, kind="ExternalInput").ap()
    qsel = nc.dram_tensor("qsel", [P, 4], F32, kind="ExternalInput").ap()
    outT = nc.dram_tensor("outT", [D, T], F32, kind="ExternalOutput").ap()
    cc_in = nc.dram_tensor("cc_in", [P, 32], F32)
    cc_out = nc.dram_tensor("cc_out", [4 * P, 32], F32)

    xq3 = xq.rearrange("(c p) t -> p c t", p=P)
    out3 = outT.rearrange("(c p) t -> p c t", p=P)
    ccg = cc_out.rearrange("(r p) j -> p r j", p=P)

    with ExitStack() as ctx:
        AR = ctx.enter_context(nc.sbuf_tensor("AR", [P, TOTAL_BYTES // 2], BF16))

        def cv_f32(off_b, rows, cols):
            v = AR[:, off_b // 2:(off_b + cols * 4) // 2].bitcast(F32)
            return v[0:rows, :] if rows != P else v

        def cv_bf(off_b, rows, cols):
            v = AR[:, off_b // 2:(off_b + cols * 2) // 2]
            return v[0:rows, :] if rows != P else v

        SM = cv_f32(OFF_SM, P, 184)
        SM2 = cv_f32(OFF_SM2, P, 16)
        QS = cv_f32(OFF_QS, P, 4)
        MBOX = cv_f32(OFF_MBOX, P, 32)
        MG = cv_f32(OFF_MG, P, 128)
        CARRY = cv_f32(OFF_CARRY, P, 16)
        CT1 = cv_f32(OFF_CT1, P, 16)
        CT2 = cv_f32(OFF_CT2, P, 16)
        ONESR = cv_f32(OFF_ONESR, 1, 128)
        ONESRB = cv_bf(OFF_ONESRB, 1, 128)
        ONESCB = cv_bf(OFF_ONESCB, P, 1)
        EPS = cv_f32(OFF_EPS, 1, 1)
        CROWD = cv_bf(OFF_CROWD, 1, 1024)
        H = cv_bf(OFF_H, P, C * T)
        XN = cv_f32(OFF_BIG, P, C * T)

        def z1t(c):
            return cv_bf(OFF_BIG + c * 2048, P, T)

        def z2t(c):
            return cv_bf(OFF_BIG + 32768 + c * 2048, P, T)

        def yt(c):
            return cv_bf(OFF_BIG + 65536 + c * 2048, P, T)

        def gut(j):
            if j < 16:
                return yt(j)
            return cv_bf(OFF_A7 + (j - 16) * 2048, P, T)

        WS = [cv_bf(OFF_WS + j * 4096, P, 2048) for j in range(6)]
        WD = [cv_bf(OFF_A3, P, SEGH * 128), cv_bf(OFF_A5, P, SEGH * 128)]
        XS = [cv_f32(OFF_A1, P, T), cv_f32(OFF_A2, P, T), cv_f32(OFF_A4, P, T)]
        OUTB = XS
        SQB = [cv_bf(OFF_A3, P, T), cv_bf(OFF_A3 + 2048, P, T)]
        ABC = cv_f32(OFF_A3, P, T)
        BBC = cv_f32(OFF_A4, P, T)
        SCRD = cv_f32(OFF_A3, P, T)
        SCRB = cv_f32(OFF_A4, P, T)
        SCRT = cv_f32(OFF_A5, P, T)
        ZEROS = cv_f32(OFF_A6, P, T)
        STATES = cv_f32(OFF_A5, P, T)
        GPS = cv_f32(OFF_A7, P, T)
        T1 = cv_f32(OFF_A7, 1, 1024)
        T2 = cv_f32(OFF_A7 + 4096, 1, 1024)
        ROW3 = cv_f32(OFF_A7 + 8192, 1, 1024)
        AROW = cv_f32(OFF_A7 + 12288, 1, 1024)
        BROW = cv_f32(OFF_A7 + 16384, 1, 1024)
        GTMP = cv_f32(OFF_A7 + 12288, P, T)

        PS = [ctx.enter_context(nc.psum_tensor(f"PS_{i}", [P, 1024], F32))
              for i in range(4)]

        sem_names = (["ld", "mb", "cc", "mg", "pe", "dv", "ac", "gp",
                      "xs0", "xs1", "xs2", "wd0", "wd1", "so0", "so1"]
                     + [f"ws{j}" for j in range(6)])
        SEMS = {n: ctx.enter_context(nc.semaphore(n)) for n in sem_names}
        CNT = {n: 0 for n in sem_names}

        SCHED = {"sync": [], "gpsimd": [], "vector": [], "scalar": [],
                 "tensor": []}

        def emit(eng, fn):
            SCHED[eng].append(fn)

        def wait(eng, sem, val):
            if val <= 0:
                return
            emit(eng, lambda e, h=SEMS[sem], v=val: e.wait_ge(h, v))

        def op(eng, meth, *args, inc=None, incby=1, **kw):
            if eng in ("vector", "gpsimd") and meth != "dma_start":
                esem = "dv" if eng == "vector" else "gp"
                wait(eng, esem, CNT[esem])
                if inc is None:
                    inc, incby = esem, 1
            if inc:
                CNT[inc] += incby
            h = SEMS[inc] if inc else None
            def fn(e, meth=meth, args=args, kw=kw, h=h, incby=incby):
                r = getattr(e, meth)(*args, **kw)
                if h is not None:
                    r.then_inc(h, incby)
            emit(eng, fn)
            return CNT[inc] if inc else None

        def dma(queue, out, in_, sem):
            return op(queue, "dma_start", inc=sem, incby=16, out=out, in_=in_)

        ws_free = [("pe", 0)] * 6
        ws_ready = [0] * 6

        def load_stripe(j, src_ap):
            s, v = ws_free[j]
            wait("gpsimd", s, v)
            ws_ready[j] = dma("gpsimd", WS[j][:, :], src_ap, f"ws{j}")

        dma("sync", SM, smalls, "ld")
        dma("sync", QS, qsel, "ld")
        op("vector", "memset", ONESR, 1.0)
        op("vector", "memset", ONESRB, 1.0)
        op("vector", "memset", ONESCB, 1.0)
        op("vector", "memset", EPS, EPSV)
        op("vector", "memset", ZEROS, 0.0)
        wait("vector", "ld", 32)
        op("vector", "tensor_tensor", SM2, SM[:, 16:32], SM[:, 32:48], ALU.add)

        ph0_done = [0] * C
        cast_done = [0] * C
        sq_done = [0] * C
        xs_ready = [0, 0, 0]
        for c in range(C):
            j = c % 3
            if c >= 3:
                wait("sync", "dv", cast_done[c - 3])
            xs_ready[j] = dma("sync", XS[j], xq3[:, c, :], f"xs{j}")
            wait("vector", f"xs{j}", xs_ready[j])
            cast_done[c] = op("vector", "tensor_copy", H[:, c * T:(c + 1) * T],
                              XS[j], inc="dv")
            wait("scalar", "dv", cast_done[c])
            if c >= 2:
                wait("scalar", "pe", ph0_done[c - 2])
            sq_done[c] = op("scalar", "activation", SQB[c % 2],
                            H[:, c * T:(c + 1) * T], AF.Square,
                            bias=0.0, scale=1.0, inc="ac")
            wait("tensor", "dv", cast_done[c])
            for n in range(2):
                op("tensor", "matmul", PS[0][0:1, n * NB:(n + 1) * NB], ONESCB,
                   H[:, c * T + n * NB:c * T + n * NB + NB], start=(c == 0),
                   stop=(c == C - 1))
            wait("tensor", "ac", sq_done[c])
            for n in range(2):
                ph0_done[c] = op("tensor", "matmul",
                                 PS[2][0:1, n * NB:(n + 1) * NB], ONESCB,
                                 SQB[c % 2][:, n * NB:(n + 1) * NB],
                                 start=(c == 0), stop=(c == C - 1),
                                 inc="pe" if n == 1 else None)

        def stats_rows(src_s1, src_s2):
            n1 = op("vector", "tensor_scalar", T1, src_s1, 1.0 / D, None,
                    ALU.mult, inc="dv")
            n2 = op("vector", "tensor_scalar", T2, src_s2, 1.0 / D, None,
                    ALU.mult, inc="dv")
            wait("vector", "dv", n1)
            n3 = op("vector", "tensor_tensor", ROW3, T1, T1, ALU.mult,
                    inc="dv")
            wait("vector", "dv", n3)
            dvn = op("vector", "tensor_tensor", T2, T2, ROW3, ALU.subtract,
                     inc="dv")
            wait("scalar", "dv", dvn)
            acn = op("scalar", "activation", ROW3, T2, AF.Sqrt, bias=EPS,
                     scale=1.0, inc="ac")
            wait("vector", "ac", acn)
            n5 = op("vector", "reciprocal", AROW, ROW3, inc="dv")
            wait("vector", "dv", n5)
            return op("vector", "scalar_tensor_tensor", BROW, T1, -1.0, AROW,
                      ALU.mult, ALU.mult, inc="dv")

        def broadcasts_and_copies(rows_n):
            wait("tensor", "dv", rows_n)
            for n in range(2):
                op("tensor", "matmul", PS[1][:, n * NB:(n + 1) * NB], ONESR,
                   AROW[0:1, n * NB:(n + 1) * NB], start=True, stop=True)
            bc = 0
            for n in range(2):
                bc = op("tensor", "matmul", PS[3][:, n * NB:(n + 1) * NB],
                        ONESR, BROW[0:1, n * NB:(n + 1) * NB], start=True,
                        stop=True, inc="pe" if n == 1 else None)
            wait("vector", "pe", bc)
            op("vector", "tensor_copy", ABC, PS[1][:, :])
            return op("vector", "tensor_copy", BBC, PS[3][:, :], inc="dv")

        wait("vector", "pe", ph0_done[C - 1])
        rows1 = stats_rows(PS[0][0:1, :], PS[2][0:1, :])
        abc1 = broadcasts_and_copies(rows1)

        load_stripe(0, w_in[0])
        load_stripe(1, w_dec[0])

        h1dv = [0] * C
        h1gp = [0] * C
        wait("vector", "dv", abc1)
        for c in range(C):
            hc = H[:, c * T:(c + 1) * T]
            if c % 3 == 2:
                if c == 2:
                    wait("gpsimd", "dv", abc1)
                else:
                    wait("gpsimd", "gp", CNT["gp"])
                n = op("gpsimd", "tensor_tensor", GPS, hc, ABC, ALU.mult,
                       inc="gp")
                wait("gpsimd", "gp", n)
                h1gp[c] = op("gpsimd", "tensor_tensor", hc, GPS, BBC,
                             ALU.add, inc="gp")
            else:
                n = op("vector", "tensor_tensor", SCRT, hc, ABC, ALU.mult,
                       inc="dv")
                wait("vector", "dv", n)
                h1dv[c] = op("vector", "tensor_tensor", hc, SCRT, BBC,
                             ALU.add, inc="dv")

        def wait_h1(eng, c):
            if c % 3 == 2:
                wait(eng, "gp", h1gp[c])
            else:
                wait(eng, "dv", h1dv[c])

        X1B = XS[0]
        pe_inp = [0] * C
        pe_tile = [0] * C
        sig_inp = [0] * C
        act_tile = [0] * C
        stt_inp = [0] * C
        dv_tile = [0] * C
        for c in range(C):
            ji, jd = (2 * c) % 6, (2 * c + 1) % 6
            if c >= 1:
                load_stripe(ji, w_in[c])
                load_stripe(jd, w_dec[c])
            p0, p1 = (PS[0], PS[1]) if c % 2 == 0 else (PS[2], PS[3])
            wait("tensor", f"ws{ji}", ws_ready[ji])
            wait("tensor", f"ws{jd}", ws_ready[jd])
            if c >= 2:
                wait("tensor", "ac", act_tile[c - 2])
                wait("tensor", "dv", stt_inp[c - 2])
            for n in range(2):
                for k in range(C):
                    if c == 0 and n == 0:
                        wait_h1("tensor", k)
                    pe_inp[c] = op(
                        "tensor", "matmul", p0[:, n * NB:(n + 1) * NB],
                        WS[ji][:, k * 128:(k + 1) * 128],
                        H[:, k * T + n * NB:k * T + n * NB + NB],
                        start=(k == 0), stop=(k == C - 1),
                        inc="pe" if (n == 1 and k == C - 1) else None)
            for n in range(2):
                for k in range(C):
                    pe_tile[c] = op(
                        "tensor", "matmul", p1[:, n * NB:(n + 1) * NB],
                        WS[jd][:, k * 128:(k + 1) * 128],
                        H[:, k * T + n * NB:k * T + n * NB + NB],
                        start=(k == 0), stop=(k == C - 1),
                        inc="pe" if (n == 1 and k == C - 1) else None)
            ws_free[ji] = ("pe", pe_tile[c])
            ws_free[jd] = ("pe", pe_tile[c])

            if c >= 1:
                wait("scalar", "dv", dv_tile[c - 1])
            wait("scalar", "pe", pe_inp[c])
            sig_inp[c] = op("scalar", "activation", SCRB, p0[:, :],
                            AF.Sigmoid,
                            bias=SM[:, SM_BIN + c:SM_BIN + c + 1], scale=1.0,
                            inc="ac")
            wait("scalar", "pe", pe_tile[c])
            act_tile[c] = op("scalar", "activation", SCRD, p1[:, :],
                             AF.Sigmoid, bias=SM2[:, c:c + 1], scale=1.0,
                             inc="ac")

            wait("vector", "ac", sig_inp[c])
            stt_inp[c] = op("vector", "scalar_tensor_tensor", SCRT, p0[:, :],
                            SM[:, SM_BIN + c:SM_BIN + c + 1], SCRB,
                            ALU.add, ALU.mult, inc="dv")
            wait("vector", "ac", act_tile[c])
            op("vector", "tensor_tensor", SCRB, SCRD, SCRT, ALU.mult)
            op("vector", "tensor_tensor", SCRT, SCRT, SCRB, ALU.subtract)
            op("vector", "tensor_tensor_scan", SCRB, SCRD, SCRT, 0.0,
               ALU.mult, ALU.add, inc="dv")
            op("vector", "tensor_tensor_scan", SCRT, SCRD, ZEROS, 1.0,
               ALU.mult, ALU.add, inc="dv")
            op("vector", "tensor_copy", MBOX[:, c:c + 1], SCRB[:, T - 1:T])
            op("vector", "tensor_copy", MBOX[:, 16 + c:17 + c],
               SCRT[:, T - 1:T])
            op("vector", "tensor_copy", z1t(c), SCRB)
            dv_tile[c] = op("vector", "tensor_copy", z2t(c), SCRT, inc="dv")

        for c0 in (0, 1):
            load_stripe((32 + c0) % 6, w_gate[c0])
        wait("sync", "dv", dv_tile[C - 1])
        mb = dma("sync", cc_in.ap(), MBOX, "mb")
        wait("gpsimd", "mb", mb)
        op("gpsimd", "collective_compute", "AllGather", ALU.bypass,
           replica_groups=[[0, 1, 2, 3], [4, 5, 6, 7]],
           ins=[cc_in.ap().opt()], outs=[cc_out.ap().opt()], inc="cc")
        wait("sync", "cc", CNT["cc"])
        mg = dma("sync", MG.rearrange("p (r j) -> p r j", j=32), ccg, "mg")
        wait("vector", "mg", mg)
        op("vector", "memset", CARRY, 0.0)
        op("vector", "memset", CT1, 0.0)
        for r in range(3):
            op("vector", "tensor_tensor", CT2, MG[:, r * 32 + 16:r * 32 + 32],
               CT1, ALU.mult)
            op("vector", "tensor_tensor", CT1, MG[:, r * 32:r * 32 + 16], CT2,
               ALU.add)
            op("vector", "scalar_tensor_tensor", CARRY, CT1,
               QS[:, r + 1:r + 2], CARRY, ALU.mult, ALU.add)

        pe_gate = [0] * C
        gate_evac = [0] * C
        corr = [None] * C
        for c in range(C):
            jg = (32 + c) % 6
            if c >= 2:
                load_stripe(jg, w_gate[c])
            pg = PS[0] if c % 2 == 0 else PS[2]
            wait("tensor", f"ws{jg}", ws_ready[jg])
            if c >= 2:
                wait("tensor", "ac", gate_evac[c - 2])
            for n in range(2):
                for k in range(C):
                    pe_gate[c] = op(
                        "tensor", "matmul", pg[:, n * NB:(n + 1) * NB],
                        WS[jg][:, k * 128:(k + 1) * 128],
                        H[:, k * T + n * NB:k * T + n * NB + NB],
                        start=(k == 0), stop=(k == C - 1),
                        inc="pe" if (n == 1 and k == C - 1) else None)
            ws_free[jg] = ("pe", pe_gate[c])

            wait("scalar", "pe", pe_gate[c])
            gate_evac[c] = op("scalar", "activation", yt(c), pg[:, :],
                              AF.Sigmoid,
                              bias=SM[:, SM_BGATE + c:SM_BGATE + c + 1],
                              scale=1.0, inc="ac")

            wait("vector", "ac", gate_evac[c])
            op("vector", "scalar_tensor_tensor", STATES, z2t(c),
               CARRY[:, c:c + 1], z1t(c), ALU.mult, ALU.add)
            n = op("vector", "tensor_tensor", yt(c), yt(c), STATES, ALU.mult)
            corr[c] = ("dv", n)

        pe_out = [0] * C
        d_evac = [0] * C
        cast2_done = [0] * C
        sq2_done = [0] * C
        e1_done = [0] * C
        for e in range(C):
            jo = (48 + e) % 6
            load_stripe(jo, w_out[e])
            j = e % 2
            if e >= 2:
                wait("sync", "dv", d_evac[e - 2])
            else:
                wait("sync", "dv", cast_done[15 if e == 0 else 13])
            xs_ready[j] = dma("sync", XS[j], xq3[:, e, :], f"xs{j}")

            po = PS[0] if e % 2 == 0 else PS[2]
            wait("tensor", f"ws{jo}", ws_ready[jo])
            if e >= 2:
                wait("tensor", "dv", d_evac[e - 2])
            for n in range(2):
                for c in range(C):
                    if e == 0 and n == 0:
                        wait("tensor", corr[c][0], corr[c][1])
                    pe_out[e] = op(
                        "tensor", "matmul", po[:, n * NB:(n + 1) * NB],
                        WS[jo][:, c * 128:(c + 1) * 128],
                        yt(c)[:, n * NB:(n + 1) * NB],
                        start=(c == 0), stop=(c == C - 1),
                        inc="pe" if (n == 1 and c == C - 1) else None)
            ws_free[jo] = ("pe", pe_out[e])

            wait("vector", "pe", pe_out[e])
            wait("vector", f"xs{j}", xs_ready[j])
            d_evac[e] = op("vector", "scalar_tensor_tensor",
                           XN[:, e * T:(e + 1) * T], po[:, :],
                           SM[:, SM_BOUT + e:SM_BOUT + e + 1], XS[j],
                           ALU.add, ALU.add, inc="dv")
            cast2_done[e] = op("vector", "tensor_copy",
                               H[:, e * T:(e + 1) * T],
                               XN[:, e * T:(e + 1) * T], inc="dv")
            wait("scalar", "dv", cast2_done[e])
            if e >= 2:
                wait("scalar", "pe", e1_done[e - 2])
            sq2_done[e] = op("scalar", "activation", SQB[j],
                             H[:, e * T:(e + 1) * T], AF.Square, bias=0.0,
                             scale=1.0, inc="ac")

            def emit_stats(e2):
                j2 = e2 % 2
                wait("tensor", "dv", cast2_done[e2])
                for n in range(2):
                    op("tensor", "matmul", PS[1][0:1, n * NB:(n + 1) * NB],
                       ONESCB, H[:, e2 * T + n * NB:e2 * T + n * NB + NB],
                       start=(e2 == 0), stop=(e2 == C - 1))
                wait("tensor", "ac", sq2_done[e2])
                for n in range(2):
                    e1_done[e2] = op("tensor", "matmul",
                                     PS[3][0:1, n * NB:(n + 1) * NB], ONESCB,
                                     SQB[j2][:, n * NB:(n + 1) * NB],
                                     start=(e2 == 0), stop=(e2 == C - 1),
                                     inc="pe" if n == 1 else None)

            if e >= 1:
                emit_stats(e - 1)
            if e == C - 1:
                emit_stats(e)

        wait("vector", "pe", e1_done[C - 1])
        rows2 = stats_rows(PS[1][0:1, :], PS[3][0:1, :])
        abc2 = broadcasts_and_copies(rows2)

        for h0 in (0, 1):
            load_stripe((2 * h0) % 6, wg_t[h0])
            load_stripe((2 * h0 + 1) % 6, wu_t[h0])

        h2dv = [0] * C
        h2gp = [0] * C
        wait("vector", "dv", abc2)
        for c in range(C):
            hc = H[:, c * T:(c + 1) * T]
            xc = XN[:, c * T:(c + 1) * T]
            if c % 3 == 2:
                if c == 2:
                    wait("gpsimd", "dv", abc2)
                n = op("gpsimd", "tensor_tensor", GPS, xc, ABC, ALU.mult)
                h2gp[c] = op("gpsimd", "tensor_tensor", hc, GPS, BBC,
                             ALU.add, inc="gp")
            else:
                n = op("vector", "tensor_tensor", SCRT, xc, ABC, ALU.mult)
                h2dv[c] = op("vector", "tensor_tensor", hc, SCRT, BBC,
                             ALU.add, inc="dv")

        def wait_h2(eng, c):
            if c % 3 == 2:
                wait(eng, "gp", h2gp[c])
            else:
                wait(eng, "dv", h2dv[c])

        wd_free = [("dv", h2dv[15]), ("dv", h2dv[15])]
        wd_ready = [0, 0]
        so_cnt = [0, 0]
        outb_free = [0, 0]
        prev_devac15 = 0
        prev_guevac = None
        for seg in range(NSEG):
            g_done = [0] * SEGH
            u_done = [0] * SEGH
            gevac = [0] * SEGH
            guevac = [0] * SEGH
            for jj in range(SEGH):
                h = seg * SEGH + jj
                jg, ju = (2 * h) % 6, (2 * h + 1) % 6
                if not (seg == 0 and jj < 2):
                    load_stripe(jg, wg_t[h])
                    load_stripe(ju, wu_t[h])
                pg = PS[0] if jj % 2 == 0 else PS[2]
                pu = PS[1] if jj % 2 == 0 else PS[3]
                wait("tensor", f"ws{jg}", ws_ready[jg])
                wait("tensor", f"ws{ju}", ws_ready[ju])
                if seg == 0 and jj == 0:
                    wait("tensor", "dv", abc2)
                if seg == 1 and jj == 0:
                    wait("tensor", "dv", prev_devac15)
                if jj >= 2:
                    wait("tensor", "ac", gevac[jj - 2])
                    wait("tensor", "dv", guevac[jj - 2])
                for n in range(2):
                    for k in range(C):
                        if seg == 0 and jj == 0 and n == 0:
                            wait_h2("tensor", k)
                        g_done[jj] = op(
                            "tensor", "matmul", pg[:, n * NB:(n + 1) * NB],
                            WS[jg][:, k * 128:(k + 1) * 128],
                            H[:, k * T + n * NB:k * T + n * NB + NB],
                            start=(k == 0), stop=(k == C - 1),
                            inc="pe" if (n == 1 and k == C - 1) else None)
                for n in range(2):
                    for k in range(C):
                        u_done[jj] = op(
                            "tensor", "matmul", pu[:, n * NB:(n + 1) * NB],
                            WS[ju][:, k * 128:(k + 1) * 128],
                            H[:, k * T + n * NB:k * T + n * NB + NB],
                            start=(k == 0), stop=(k == C - 1),
                            inc="pe" if (n == 1 and k == C - 1) else None)
                ws_free[jg] = ("pe", u_done[jj])
                ws_free[ju] = ("pe", u_done[jj])

                wait("scalar", "pe", g_done[jj])
                if jj >= 1:
                    wait("scalar", "dv", guevac[jj - 1])
                elif seg == 1:
                    wait("scalar", "dv", prev_guevac[SEGH - 1])
                gevac[jj] = op("scalar", "activation", GTMP, pg[:, :],
                               AF.Silu, bias=0.0,
                               scale=SM[:, SM_SGATE + h:SM_SGATE + h + 1],
                               inc="ac")

                wait("vector", "pe", u_done[jj])
                wait("vector", "ac", gevac[jj])
                guevac[jj] = op("vector", "scalar_tensor_tensor", gut(jj),
                                pu[:, :], SM[:, SM_SUP + h:SM_SUP + h + 1],
                                GTMP, ALU.mult, ALU.mult, inc="dv")

            wait("tensor", "dv", guevac[SEGH - 1])
            csg = 0
            for jj in range(SEGH):
                for n in range(2):
                    csg = op("tensor", "matmul",
                             PS[1][0:1, n * NB:(n + 1) * NB], ONESCB,
                             gut(jj)[:, n * NB:(n + 1) * NB],
                             start=(jj == 0), stop=(jj == SEGH - 1),
                             inc="pe" if (jj == SEGH - 1 and n == 1) else None)
            wait("vector", "pe", csg)
            crowd_done = op("vector", "tensor_scalar", CROWD, PS[1][0:1, :],
                            -128.0, None, ALU.mult, inc="dv")

            down_done = [0] * C
            devac = [0] * C
            for d in range(C):
                j = d % 2
                s, v = wd_free[j]
                wait("gpsimd", s, v)
                wd_ready[j] = dma("gpsimd", WD[j][:, :], wd_t[seg, d],
                                  f"wd{j}")

                pd = PS[0] if d % 2 == 0 else PS[2]
                wait("tensor", f"wd{j}", wd_ready[j])
                if d == 0:
                    wait("tensor", "dv", crowd_done)
                if d >= 2:
                    wait("tensor", "dv", devac[d - 2])
                for n in range(2):
                    for jj in range(SEGH):
                        op("tensor", "matmul", pd[:, n * NB:(n + 1) * NB],
                           WD[j][:, jj * 128:(jj + 1) * 128],
                           gut(jj)[:, n * NB:(n + 1) * NB],
                           start=(jj == 0), stop=False)
                    down_done[d] = op("tensor", "matmul",
                                      pd[:, n * NB:(n + 1) * NB], ONESRB,
                                      CROWD[0:1, n * NB:(n + 1) * NB],
                                      start=False, stop=True,
                                      inc="pe" if n == 1 else None)
                wd_free[j] = ("pe", down_done[d])

                wait("vector", "pe", down_done[d])
                if seg == 0:
                    devac[d] = op("vector", "scalar_tensor_tensor",
                                  XN[:, d * T:(d + 1) * T], pd[:, :],
                                  SM[:, SM_SDOWN + d:SM_SDOWN + d + 1],
                                  XN[:, d * T:(d + 1) * T], ALU.mult, ALU.add,
                                  inc="dv")
                else:
                    wait("vector", f"so{j}", outb_free[j])
                    devac[d] = op("vector", "scalar_tensor_tensor", OUTB[j],
                                  pd[:, :],
                                  SM[:, SM_SDOWN + d:SM_SDOWN + d + 1],
                                  XN[:, d * T:(d + 1) * T], ALU.mult, ALU.add,
                                  inc="dv")
                    wait("sync", "dv", devac[d])
                    so_cnt[j] = dma("sync", out3[:, d, :], OUTB[j], f"so{j}")
                    outb_free[j] = so_cnt[j]
            prev_guevac = guevac
            prev_devac15 = devac[C - 1]

        wait("sync", "so0", so_cnt[0])
        wait("sync", "so1", so_cnt[1])

        with nc.Block() as block:
            @block.sync
            def _(e):
                for fn in SCHED["sync"]:
                    fn(e)

            @block.gpsimd
            def _(e):
                for fn in SCHED["gpsimd"]:
                    fn(e)

            @block.vector
            def _(e):
                for fn in SCHED["vector"]:
                    fn(e)

            @block.scalar
            def _(e):
                for fn in SCHED["scalar"]:
                    fn(e)

            @block.tensor
            def _(e):
                for fn in SCHED["tensor"]:
                    fn(e)

    return nc


def kernel(**inputs):
    x = np.asarray(inputs["L0_attn_x"], dtype=np.float32)

    def att_w(w):
        a = np.asarray(w, dtype=np.float32).reshape(C, P, C, 128)
        return np.ascontiguousarray(a.transpose(2, 1, 0, 3).reshape(C, P, 2048))

    def ffn_w(w):
        a = np.asarray(w, dtype=np.int32).reshape(HC, 128, C, P)
        return np.ascontiguousarray(a.transpose(0, 3, 2, 1).reshape(HC, P, 2048))

    def down_w(w):
        a = np.asarray(w, dtype=np.int32).reshape(C, 128, NSEG, SEGH, P)
        return np.ascontiguousarray(
            a.transpose(2, 0, 4, 3, 1).reshape(NSEG, C, P, SEGH * 128))

    w_in = att_w(inputs["L0_attn_W_in"])
    w_dec = att_w(inputs["L0_attn_W_decay"])
    w_gate = att_w(inputs["L0_attn_W_gate"])
    w_out = att_w(inputs["L0_attn_W_out"])
    wg_t = ffn_w(inputs["L0_ffn_wq_gate"])
    wu_t = ffn_w(inputs["L0_ffn_wq_up"])
    wd_t = down_w(inputs["L0_ffn_wq_down"])

    def rs(v):
        return np.asarray(v, dtype=np.float32).reshape(-1, 128).T

    smalls = np.ascontiguousarray(np.concatenate(
        [rs(inputs["L0_attn_b_in"]), rs(inputs["L0_attn_b_decay"]),
         rs(inputs["L0_attn_decay_bias"]), rs(inputs["L0_attn_b_gate"]),
         rs(inputs["L0_attn_b_out"]), rs(inputs["L0_ffn_s_down"]),
         rs(inputs["L0_ffn_s_gate"]), rs(inputs["L0_ffn_s_up"])], axis=1))

    xT = np.ascontiguousarray(np.transpose(x, (0, 2, 1)))

    nc = build_program()
    in_maps = []
    for core in range(8):
        b, q = core // 4, core % 4
        qs = np.zeros((P, 4), dtype=np.float32)
        qs[:, q] = 1.0
        in_maps.append(dict(
            xq=np.ascontiguousarray(xT[b][:, q * T:(q + 1) * T]),
            w_in=w_in, w_dec=w_dec, w_gate=w_gate, w_out=w_out,
            wg_t=wg_t, wu_t=wu_t, wd_t=wd_t, smalls=smalls, qsel=qs))

    res = run_bass_kernel_spmd(nc, in_maps, core_ids=list(range(8)))

    outT = np.empty((2, D, 4096), dtype=np.float32)
    for core in range(8):
        b, q = core // 4, core % 4
        outT[b][:, q * T:(q + 1) * T] = np.asarray(res.results[core]["outT"])
    return np.ascontiguousarray(np.transpose(outT, (0, 2, 1)))


# version 08
# speedup vs baseline: 1.0993x; 1.0058x over previous
import numpy as np
import concourse.bass as bass
import concourse.mybir as mybir
from concourse.bass_utils import run_bass_kernel_spmd
from contextlib import ExitStack

F32 = mybir.dt.float32
BF16 = mybir.dt.bfloat16
I32 = mybir.dt.int32
AF = mybir.ActivationFunctionType
ALU = mybir.AluOpType

P = 128
T = 1024
C = 16
D = 2048
HID = 5632
HC = 44
SEGH = 22
NSEG = 2
NB = 512
EPSV = 1e-5

OFF_SM = 0
OFF_SM2 = 736
OFF_QS = 800
OFF_MBOX = 816
OFF_MG = 944
OFF_CARRY = 1456
OFF_CT1 = 1520
OFF_CT2 = 1584
OFF_ONESR = 1648
OFF_ONESRB = 2160
OFF_ONESC = 2416
OFF_ONESCB = 2420
OFF_EPS = 2424
OFF_CROWD = 4480
OFF_H = 6656
OFF_BIG = 39424
OFF_WS = 137728
OFF_A1 = 162304
OFF_A2 = 166400
OFF_A3 = 170496
OFF_A4 = 174592
OFF_A5 = 178688
OFF_A6 = 182784
OFF_A7 = 186880
TOTAL_BYTES = 207360

SM_BIN = 0
SM_BDEC = 16
SM_DBIAS = 32
SM_BGATE = 48
SM_BOUT = 64
SM_SDOWN = 80
SM_SGATE = 96
SM_SUP = 140


def build_program():
    nc = bass.Bass("TRN2", target_bir_lowering=False)

    xq = nc.dram_tensor("xq", [D, T], F32, kind="ExternalInput").ap()
    w_in = nc.dram_tensor("w_in", [C, P, 2048], F32, kind="ExternalInput").ap()
    w_dec = nc.dram_tensor("w_dec", [C, P, 2048], F32, kind="ExternalInput").ap()
    w_gate = nc.dram_tensor("w_gate", [C, P, 2048], F32, kind="ExternalInput").ap()
    w_out = nc.dram_tensor("w_out", [C, P, 2048], F32, kind="ExternalInput").ap()
    wg_t = nc.dram_tensor("wg_t", [HC, P, 2048], I32, kind="ExternalInput").ap()
    wu_t = nc.dram_tensor("wu_t", [HC, P, 2048], I32, kind="ExternalInput").ap()
    wd_t = nc.dram_tensor("wd_t", [NSEG, C, P, SEGH * 128], I32,
                          kind="ExternalInput").ap()
    smalls = nc.dram_tensor("smalls", [P, 184], F32, kind="ExternalInput").ap()
    qsel = nc.dram_tensor("qsel", [P, 4], F32, kind="ExternalInput").ap()
    outT = nc.dram_tensor("outT", [D, T], F32, kind="ExternalOutput").ap()
    cc_in = nc.dram_tensor("cc_in", [P, 32], F32)
    cc_out = nc.dram_tensor("cc_out", [4 * P, 32], F32)

    xq3 = xq.rearrange("(c p) t -> p c t", p=P)
    out3 = outT.rearrange("(c p) t -> p c t", p=P)
    ccg = cc_out.rearrange("(r p) j -> p r j", p=P)

    with ExitStack() as ctx:
        AR = ctx.enter_context(nc.sbuf_tensor("AR", [P, TOTAL_BYTES // 2], BF16))

        def cv_f32(off_b, rows, cols):
            v = AR[:, off_b // 2:(off_b + cols * 4) // 2].bitcast(F32)
            return v[0:rows, :] if rows != P else v

        def cv_bf(off_b, rows, cols):
            v = AR[:, off_b // 2:(off_b + cols * 2) // 2]
            return v[0:rows, :] if rows != P else v

        SM = cv_f32(OFF_SM, P, 184)
        SM2 = cv_f32(OFF_SM2, P, 16)
        QS = cv_f32(OFF_QS, P, 4)
        MBOX = cv_f32(OFF_MBOX, P, 32)
        MG = cv_f32(OFF_MG, P, 128)
        CARRY = cv_f32(OFF_CARRY, P, 16)
        CT1 = cv_f32(OFF_CT1, P, 16)
        CT2 = cv_f32(OFF_CT2, P, 16)
        ONESR = cv_f32(OFF_ONESR, 1, 128)
        ONESRB = cv_bf(OFF_ONESRB, 1, 128)
        ONESCB = cv_bf(OFF_ONESCB, P, 1)
        EPS = cv_f32(OFF_EPS, 1, 1)
        CROWD = cv_bf(OFF_CROWD, 1, 1024)
        H = cv_bf(OFF_H, P, C * T)
        XN = cv_f32(OFF_BIG, P, C * T)

        def z1t(c):
            return cv_bf(OFF_BIG + c * 2048, P, T)

        def z2t(c):
            return cv_bf(OFF_BIG + 32768 + c * 2048, P, T)

        def yt(c):
            return cv_bf(OFF_BIG + 65536 + c * 2048, P, T)

        def gut(j):
            if j < 16:
                return yt(j)
            return cv_bf(OFF_A7 + (j - 16) * 2048, P, T)

        WS = [cv_bf(OFF_WS + j * 4096, P, 2048) for j in range(6)]
        WD = [cv_bf(OFF_A3, P, SEGH * 128), cv_bf(OFF_A5, P, SEGH * 128)]
        XS = [cv_f32(OFF_A1, P, T), cv_f32(OFF_A2, P, T), cv_f32(OFF_A4, P, T)]
        OUTB = XS
        SQB = [cv_bf(OFF_A3, P, T), cv_bf(OFF_A3 + 2048, P, T)]
        ABC = cv_f32(OFF_A3, P, T)
        BBC = cv_f32(OFF_A4, P, T)
        SCRD = cv_f32(OFF_A3, P, T)
        SCRB = cv_f32(OFF_A4, P, T)
        SCRT = cv_f32(OFF_A5, P, T)
        ZEROS = cv_f32(OFF_A6, P, T)
        STATES = cv_f32(OFF_A5, P, T)
        GPS = cv_f32(OFF_A7, P, T)
        T1 = cv_f32(OFF_A7, 1, 1024)
        T2 = cv_f32(OFF_A7 + 4096, 1, 1024)
        ROW3 = cv_f32(OFF_A7 + 8192, 1, 1024)
        AROW = cv_f32(OFF_A7 + 12288, 1, 1024)
        BROW = cv_f32(OFF_A7 + 16384, 1, 1024)
        GTMP = cv_f32(OFF_A7 + 12288, P, T)

        PS = [ctx.enter_context(nc.psum_tensor(f"PS_{i}", [P, 1024], F32))
              for i in range(4)]

        sem_names = (["ld", "mb", "cc", "mg", "pe", "dv", "ac", "gp",
                      "xs0", "xs1", "xs2", "wd0", "wd1", "so0", "so1"]
                     + [f"ws{j}" for j in range(6)])
        SEMS = {n: ctx.enter_context(nc.semaphore(n)) for n in sem_names}
        CNT = {n: 0 for n in sem_names}

        SCHED = {"sync": [], "gpsimd": [], "vector": [], "scalar": [],
                 "tensor": []}

        def emit(eng, fn):
            SCHED[eng].append(fn)

        def wait(eng, sem, val):
            if val <= 0:
                return
            emit(eng, lambda e, h=SEMS[sem], v=val: e.wait_ge(h, v))

        def op(eng, meth, *args, inc=None, incby=1, **kw):
            if eng in ("vector", "gpsimd") and meth != "dma_start":
                esem = "dv" if eng == "vector" else "gp"
                wait(eng, esem, CNT[esem])
                if inc is None:
                    inc, incby = esem, 1
            if inc:
                CNT[inc] += incby
            h = SEMS[inc] if inc else None
            def fn(e, meth=meth, args=args, kw=kw, h=h, incby=incby):
                r = getattr(e, meth)(*args, **kw)
                if h is not None:
                    r.then_inc(h, incby)
            emit(eng, fn)
            return CNT[inc] if inc else None

        def dma(queue, out, in_, sem):
            return op(queue, "dma_start", inc=sem, incby=16, out=out, in_=in_)

        ws_free = [("pe", 0)] * 6
        ws_ready = [0] * 6

        def load_stripe(j, src_ap):
            s, v = ws_free[j]
            wait("gpsimd", s, v)
            ws_ready[j] = dma("gpsimd", WS[j][:, :], src_ap, f"ws{j}")

        dma("sync", SM, smalls, "ld")
        dma("sync", QS, qsel, "ld")
        op("vector", "memset", ONESR, 1.0)
        op("vector", "memset", ONESRB, 1.0)
        op("vector", "memset", ONESCB, 1.0)
        op("vector", "memset", EPS, EPSV)
        op("vector", "memset", ZEROS, 0.0)
        wait("vector", "ld", 32)
        op("vector", "tensor_tensor", SM2, SM[:, 16:32], SM[:, 32:48], ALU.add)

        ph0_done = [0] * C
        hcast = [0] * C
        sq_done = [0] * C
        for c in range(C):
            hcast[c] = dma("gpsimd", H[:, c * T:(c + 1) * T], xq3[:, c, :],
                           "xs0")
        for c in range(C):
            wait("scalar", "xs0", hcast[c])
            if c >= 2:
                wait("scalar", "pe", ph0_done[c - 2])
            sq_done[c] = op("scalar", "activation", SQB[c % 2],
                            H[:, c * T:(c + 1) * T], AF.Square,
                            bias=0.0, scale=1.0, inc="ac")
            wait("tensor", "xs0", hcast[c])
            for n in range(2):
                op("tensor", "matmul", PS[0][0:1, n * NB:(n + 1) * NB], ONESCB,
                   H[:, c * T + n * NB:c * T + n * NB + NB], start=(c == 0),
                   stop=(c == C - 1))
            wait("tensor", "ac", sq_done[c])
            for n in range(2):
                ph0_done[c] = op("tensor", "matmul",
                                 PS[2][0:1, n * NB:(n + 1) * NB], ONESCB,
                                 SQB[c % 2][:, n * NB:(n + 1) * NB],
                                 start=(c == 0), stop=(c == C - 1),
                                 inc="pe" if n == 1 else None)

        def stats_rows(src_s1, src_s2):
            n1 = op("vector", "tensor_scalar", T1, src_s1, 1.0 / D, None,
                    ALU.mult, inc="dv")
            n2 = op("vector", "tensor_scalar", T2, src_s2, 1.0 / D, None,
                    ALU.mult, inc="dv")
            wait("vector", "dv", n1)
            n3 = op("vector", "tensor_tensor", ROW3, T1, T1, ALU.mult,
                    inc="dv")
            wait("vector", "dv", n3)
            dvn = op("vector", "tensor_tensor", T2, T2, ROW3, ALU.subtract,
                     inc="dv")
            wait("scalar", "dv", dvn)
            acn = op("scalar", "activation", ROW3, T2, AF.Sqrt, bias=EPS,
                     scale=1.0, inc="ac")
            wait("vector", "ac", acn)
            n5 = op("vector", "reciprocal", AROW, ROW3, inc="dv")
            wait("vector", "dv", n5)
            return op("vector", "scalar_tensor_tensor", BROW, T1, -1.0, AROW,
                      ALU.mult, ALU.mult, inc="dv")

        def broadcasts_and_copies(rows_n):
            wait("tensor", "dv", rows_n)
            for n in range(2):
                op("tensor", "matmul", PS[1][:, n * NB:(n + 1) * NB], ONESR,
                   AROW[0:1, n * NB:(n + 1) * NB], start=True, stop=True)
            bc = 0
            for n in range(2):
                bc = op("tensor", "matmul", PS[3][:, n * NB:(n + 1) * NB],
                        ONESR, BROW[0:1, n * NB:(n + 1) * NB], start=True,
                        stop=True, inc="pe" if n == 1 else None)
            wait("vector", "pe", bc)
            op("vector", "tensor_copy", ABC, PS[1][:, :])
            return op("vector", "tensor_copy", BBC, PS[3][:, :], inc="dv")

        wait("vector", "pe", ph0_done[C - 1])
        rows1 = stats_rows(PS[0][0:1, :], PS[2][0:1, :])
        abc1 = broadcasts_and_copies(rows1)

        load_stripe(0, w_in[0])
        load_stripe(1, w_dec[0])

        h1dv = [0] * C
        h1gp = [0] * C
        wait("vector", "dv", abc1)
        for c in range(C):
            hc = H[:, c * T:(c + 1) * T]
            if c % 3 == 2:
                if c == 2:
                    wait("gpsimd", "dv", abc1)
                else:
                    wait("gpsimd", "gp", CNT["gp"])
                n = op("gpsimd", "tensor_tensor", GPS, hc, ABC, ALU.mult,
                       inc="gp")
                wait("gpsimd", "gp", n)
                h1gp[c] = op("gpsimd", "tensor_tensor", hc, GPS, BBC,
                             ALU.add, inc="gp")
            else:
                n = op("vector", "tensor_tensor", SCRT, hc, ABC, ALU.mult,
                       inc="dv")
                wait("vector", "dv", n)
                h1dv[c] = op("vector", "tensor_tensor", hc, SCRT, BBC,
                             ALU.add, inc="dv")

        def wait_h1(eng, c):
            if c % 3 == 2:
                wait(eng, "gp", h1gp[c])
            else:
                wait(eng, "dv", h1dv[c])

        X1B = XS[0]
        pe_inp = [0] * C
        pe_tile = [0] * C
        sig_inp = [0] * C
        act_tile = [0] * C
        stt_inp = [0] * C
        dv_tile = [0] * C
        for c in range(C):
            ji, jd = (2 * c) % 6, (2 * c + 1) % 6
            if c >= 1:
                load_stripe(ji, w_in[c])
                load_stripe(jd, w_dec[c])
            p0, p1 = (PS[0], PS[1]) if c % 2 == 0 else (PS[2], PS[3])
            wait("tensor", f"ws{ji}", ws_ready[ji])
            wait("tensor", f"ws{jd}", ws_ready[jd])
            if c >= 2:
                wait("tensor", "ac", act_tile[c - 2])
                wait("tensor", "dv", stt_inp[c - 2])
            for n in range(2):
                for k in range(C):
                    if c == 0 and n == 0:
                        wait_h1("tensor", k)
                    pe_inp[c] = op(
                        "tensor", "matmul", p0[:, n * NB:(n + 1) * NB],
                        WS[ji][:, k * 128:(k + 1) * 128],
                        H[:, k * T + n * NB:k * T + n * NB + NB],
                        start=(k == 0), stop=(k == C - 1),
                        inc="pe" if (n == 1 and k == C - 1) else None)
            for n in range(2):
                for k in range(C):
                    pe_tile[c] = op(
                        "tensor", "matmul", p1[:, n * NB:(n + 1) * NB],
                        WS[jd][:, k * 128:(k + 1) * 128],
                        H[:, k * T + n * NB:k * T + n * NB + NB],
                        start=(k == 0), stop=(k == C - 1),
                        inc="pe" if (n == 1 and k == C - 1) else None)
            ws_free[ji] = ("pe", pe_tile[c])
            ws_free[jd] = ("pe", pe_tile[c])

            if c >= 1:
                wait("scalar", "dv", dv_tile[c - 1])
            wait("scalar", "pe", pe_inp[c])
            sig_inp[c] = op("scalar", "activation", SCRB, p0[:, :],
                            AF.Sigmoid,
                            bias=SM[:, SM_BIN + c:SM_BIN + c + 1], scale=1.0,
                            inc="ac")
            wait("scalar", "pe", pe_tile[c])
            act_tile[c] = op("scalar", "activation", SCRD, p1[:, :],
                             AF.Sigmoid, bias=SM2[:, c:c + 1], scale=1.0,
                             inc="ac")

            wait("vector", "ac", sig_inp[c])
            stt_inp[c] = op("vector", "scalar_tensor_tensor", SCRT, p0[:, :],
                            SM[:, SM_BIN + c:SM_BIN + c + 1], SCRB,
                            ALU.add, ALU.mult, inc="dv")
            wait("vector", "ac", act_tile[c])
            op("vector", "tensor_tensor", SCRB, SCRD, SCRT, ALU.mult)
            op("vector", "tensor_tensor", SCRT, SCRT, SCRB, ALU.subtract)
            op("vector", "tensor_tensor_scan", SCRB, SCRD, SCRT, 0.0,
               ALU.mult, ALU.add, inc="dv")
            op("vector", "tensor_tensor_scan", SCRT, SCRD, ZEROS, 1.0,
               ALU.mult, ALU.add, inc="dv")
            op("vector", "tensor_copy", MBOX[:, c:c + 1], SCRB[:, T - 1:T])
            op("vector", "tensor_copy", MBOX[:, 16 + c:17 + c],
               SCRT[:, T - 1:T])
            op("vector", "tensor_copy", z1t(c), SCRB)
            dv_tile[c] = op("vector", "tensor_copy", z2t(c), SCRT, inc="dv")

        for c0 in (0, 1):
            load_stripe((32 + c0) % 6, w_gate[c0])
        wait("sync", "dv", dv_tile[C - 1])
        mb = dma("sync", cc_in.ap(), MBOX, "mb")
        wait("gpsimd", "mb", mb)
        op("gpsimd", "collective_compute", "AllGather", ALU.bypass,
           replica_groups=[[0, 1, 2, 3], [4, 5, 6, 7]],
           ins=[cc_in.ap().opt()], outs=[cc_out.ap().opt()], inc="cc")
        wait("sync", "cc", CNT["cc"])
        mg = dma("sync", MG.rearrange("p (r j) -> p r j", j=32), ccg, "mg")
        wait("vector", "mg", mg)
        op("vector", "memset", CARRY, 0.0)
        op("vector", "memset", CT1, 0.0)
        for r in range(3):
            op("vector", "tensor_tensor", CT2, MG[:, r * 32 + 16:r * 32 + 32],
               CT1, ALU.mult)
            op("vector", "tensor_tensor", CT1, MG[:, r * 32:r * 32 + 16], CT2,
               ALU.add)
            op("vector", "scalar_tensor_tensor", CARRY, CT1,
               QS[:, r + 1:r + 2], CARRY, ALU.mult, ALU.add)

        pe_gate = [0] * C
        gate_evac = [0] * C
        corr = [None] * C
        for c in range(C):
            jg = (32 + c) % 6
            if c >= 2:
                load_stripe(jg, w_gate[c])
            pg = PS[0] if c % 2 == 0 else PS[2]
            wait("tensor", f"ws{jg}", ws_ready[jg])
            if c >= 2:
                wait("tensor", "ac", gate_evac[c - 2])
            for n in range(2):
                for k in range(C):
                    pe_gate[c] = op(
                        "tensor", "matmul", pg[:, n * NB:(n + 1) * NB],
                        WS[jg][:, k * 128:(k + 1) * 128],
                        H[:, k * T + n * NB:k * T + n * NB + NB],
                        start=(k == 0), stop=(k == C - 1),
                        inc="pe" if (n == 1 and k == C - 1) else None)
            ws_free[jg] = ("pe", pe_gate[c])

            wait("scalar", "pe", pe_gate[c])
            gate_evac[c] = op("scalar", "activation", yt(c), pg[:, :],
                              AF.Sigmoid,
                              bias=SM[:, SM_BGATE + c:SM_BGATE + c + 1],
                              scale=1.0, inc="ac")

            wait("vector", "ac", gate_evac[c])
            op("vector", "scalar_tensor_tensor", STATES, z2t(c),
               CARRY[:, c:c + 1], z1t(c), ALU.mult, ALU.add)
            n = op("vector", "tensor_tensor", yt(c), yt(c), STATES, ALU.mult)
            corr[c] = ("dv", n)

        pe_out = [0] * C
        d_evac = [0] * C
        xs_ready = [0, 0]
        cast2_done = [0] * C
        sq2_done = [0] * C
        e1_done = [0] * C
        for e in range(C):
            jo = (48 + e) % 6
            load_stripe(jo, w_out[e])
            j = e % 2
            if e >= 2:
                wait("sync", "dv", d_evac[e - 2])
            xs_ready[j] = dma("sync", XS[j], xq3[:, e, :], f"xs{j}")

            po = PS[0] if e % 2 == 0 else PS[2]
            wait("tensor", f"ws{jo}", ws_ready[jo])
            if e >= 2:
                wait("tensor", "dv", d_evac[e - 2])
            for n in range(2):
                for c in range(C):
                    if e == 0 and n == 0:
                        wait("tensor", corr[c][0], corr[c][1])
                    pe_out[e] = op(
                        "tensor", "matmul", po[:, n * NB:(n + 1) * NB],
                        WS[jo][:, c * 128:(c + 1) * 128],
                        yt(c)[:, n * NB:(n + 1) * NB],
                        start=(c == 0), stop=(c == C - 1),
                        inc="pe" if (n == 1 and c == C - 1) else None)
            ws_free[jo] = ("pe", pe_out[e])

            wait("vector", "pe", pe_out[e])
            wait("vector", f"xs{j}", xs_ready[j])
            d_evac[e] = op("vector", "scalar_tensor_tensor",
                           XN[:, e * T:(e + 1) * T], po[:, :],
                           SM[:, SM_BOUT + e:SM_BOUT + e + 1], XS[j],
                           ALU.add, ALU.add, inc="dv")
            cast2_done[e] = op("vector", "tensor_copy",
                               H[:, e * T:(e + 1) * T],
                               XN[:, e * T:(e + 1) * T], inc="dv")
            wait("scalar", "dv", cast2_done[e])
            if e >= 2:
                wait("scalar", "pe", e1_done[e - 2])
            sq2_done[e] = op("scalar", "activation", SQB[j],
                             H[:, e * T:(e + 1) * T], AF.Square, bias=0.0,
                             scale=1.0, inc="ac")

            def emit_stats(e2):
                j2 = e2 % 2
                wait("tensor", "dv", cast2_done[e2])
                for n in range(2):
                    op("tensor", "matmul", PS[1][0:1, n * NB:(n + 1) * NB],
                       ONESCB, H[:, e2 * T + n * NB:e2 * T + n * NB + NB],
                       start=(e2 == 0), stop=(e2 == C - 1))
                wait("tensor", "ac", sq2_done[e2])
                for n in range(2):
                    e1_done[e2] = op("tensor", "matmul",
                                     PS[3][0:1, n * NB:(n + 1) * NB], ONESCB,
                                     SQB[j2][:, n * NB:(n + 1) * NB],
                                     start=(e2 == 0), stop=(e2 == C - 1),
                                     inc="pe" if n == 1 else None)

            if e >= 1:
                emit_stats(e - 1)
            if e == C - 1:
                emit_stats(e)

        wait("vector", "pe", e1_done[C - 1])
        rows2 = stats_rows(PS[1][0:1, :], PS[3][0:1, :])
        abc2 = broadcasts_and_copies(rows2)

        for h0 in (0, 1):
            load_stripe((2 * h0) % 6, wg_t[h0])
            load_stripe((2 * h0 + 1) % 6, wu_t[h0])

        h2dv = [0] * C
        h2gp = [0] * C
        wait("vector", "dv", abc2)
        for c in range(C):
            hc = H[:, c * T:(c + 1) * T]
            xc = XN[:, c * T:(c + 1) * T]
            if c % 3 == 2:
                if c == 2:
                    wait("gpsimd", "dv", abc2)
                n = op("gpsimd", "tensor_tensor", GPS, xc, ABC, ALU.mult)
                h2gp[c] = op("gpsimd", "tensor_tensor", hc, GPS, BBC,
                             ALU.add, inc="gp")
            else:
                n = op("vector", "tensor_tensor", SCRT, xc, ABC, ALU.mult)
                h2dv[c] = op("vector", "tensor_tensor", hc, SCRT, BBC,
                             ALU.add, inc="dv")

        def wait_h2(eng, c):
            if c % 3 == 2:
                wait(eng, "gp", h2gp[c])
            else:
                wait(eng, "dv", h2dv[c])

        wd_free = [("dv", h2dv[15]), ("dv", h2dv[15])]
        wd_ready = [0, 0]
        so_cnt = [0, 0]
        outb_free = [0, 0]
        prev_devac15 = 0
        prev_guevac = None
        for seg in range(NSEG):
            g_done = [0] * SEGH
            u_done = [0] * SEGH
            gevac = [0] * SEGH
            guevac = [0] * SEGH
            for jj in range(SEGH):
                h = seg * SEGH + jj
                jg, ju = (2 * h) % 6, (2 * h + 1) % 6
                if not (seg == 0 and jj < 2):
                    load_stripe(jg, wg_t[h])
                    load_stripe(ju, wu_t[h])
                pg = PS[0] if jj % 2 == 0 else PS[2]
                pu = PS[1] if jj % 2 == 0 else PS[3]
                wait("tensor", f"ws{jg}", ws_ready[jg])
                wait("tensor", f"ws{ju}", ws_ready[ju])
                if seg == 0 and jj == 0:
                    wait("tensor", "dv", abc2)
                if seg == 1 and jj == 0:
                    wait("tensor", "dv", prev_devac15)
                if jj >= 2:
                    wait("tensor", "ac", gevac[jj - 2])
                    wait("tensor", "dv", guevac[jj - 2])
                for n in range(2):
                    for k in range(C):
                        if seg == 0 and jj == 0 and n == 0:
                            wait_h2("tensor", k)
                        g_done[jj] = op(
                            "tensor", "matmul", pg[:, n * NB:(n + 1) * NB],
                            WS[jg][:, k * 128:(k + 1) * 128],
                            H[:, k * T + n * NB:k * T + n * NB + NB],
                            start=(k == 0), stop=(k == C - 1),
                            inc="pe" if (n == 1 and k == C - 1) else None)
                for n in range(2):
                    for k in range(C):
                        u_done[jj] = op(
                            "tensor", "matmul", pu[:, n * NB:(n + 1) * NB],
                            WS[ju][:, k * 128:(k + 1) * 128],
                            H[:, k * T + n * NB:k * T + n * NB + NB],
                            start=(k == 0), stop=(k == C - 1),
                            inc="pe" if (n == 1 and k == C - 1) else None)
                ws_free[jg] = ("pe", u_done[jj])
                ws_free[ju] = ("pe", u_done[jj])

                wait("scalar", "pe", g_done[jj])
                if jj >= 1:
                    wait("scalar", "dv", guevac[jj - 1])
                elif seg == 1:
                    wait("scalar", "dv", prev_guevac[SEGH - 1])
                gevac[jj] = op("scalar", "activation", GTMP, pg[:, :],
                               AF.Silu, bias=0.0,
                               scale=SM[:, SM_SGATE + h:SM_SGATE + h + 1],
                               inc="ac")

                wait("vector", "pe", u_done[jj])
                wait("vector", "ac", gevac[jj])
                guevac[jj] = op("vector", "scalar_tensor_tensor", gut(jj),
                                pu[:, :], SM[:, SM_SUP + h:SM_SUP + h + 1],
                                GTMP, ALU.mult, ALU.mult, inc="dv")

            wait("tensor", "dv", guevac[SEGH - 1])
            csg = 0
            for jj in range(SEGH):
                for n in range(2):
                    csg = op("tensor", "matmul",
                             PS[1][0:1, n * NB:(n + 1) * NB], ONESCB,
                             gut(jj)[:, n * NB:(n + 1) * NB],
                             start=(jj == 0), stop=(jj == SEGH - 1),
                             inc="pe" if (jj == SEGH - 1 and n == 1) else None)
            wait("vector", "pe", csg)
            crowd_done = op("vector", "tensor_scalar", CROWD, PS[1][0:1, :],
                            -128.0, None, ALU.mult, inc="dv")

            down_done = [0] * C
            devac = [0] * C
            for d in range(C):
                j = d % 2
                s, v = wd_free[j]
                wait("gpsimd", s, v)
                wd_ready[j] = dma("gpsimd", WD[j][:, :], wd_t[seg, d],
                                  f"wd{j}")

                pd = PS[0] if d % 2 == 0 else PS[2]
                wait("tensor", f"wd{j}", wd_ready[j])
                if d == 0:
                    wait("tensor", "dv", crowd_done)
                if d >= 2:
                    wait("tensor", "dv", devac[d - 2])
                for n in range(2):
                    for jj in range(SEGH):
                        op("tensor", "matmul", pd[:, n * NB:(n + 1) * NB],
                           WD[j][:, jj * 128:(jj + 1) * 128],
                           gut(jj)[:, n * NB:(n + 1) * NB],
                           start=(jj == 0), stop=False)
                    down_done[d] = op("tensor", "matmul",
                                      pd[:, n * NB:(n + 1) * NB], ONESRB,
                                      CROWD[0:1, n * NB:(n + 1) * NB],
                                      start=False, stop=True,
                                      inc="pe" if n == 1 else None)
                wd_free[j] = ("pe", down_done[d])

                wait("vector", "pe", down_done[d])
                if seg == 0:
                    devac[d] = op("vector", "scalar_tensor_tensor",
                                  XN[:, d * T:(d + 1) * T], pd[:, :],
                                  SM[:, SM_SDOWN + d:SM_SDOWN + d + 1],
                                  XN[:, d * T:(d + 1) * T], ALU.mult, ALU.add,
                                  inc="dv")
                else:
                    wait("vector", f"so{j}", outb_free[j])
                    devac[d] = op("vector", "scalar_tensor_tensor", OUTB[j],
                                  pd[:, :],
                                  SM[:, SM_SDOWN + d:SM_SDOWN + d + 1],
                                  XN[:, d * T:(d + 1) * T], ALU.mult, ALU.add,
                                  inc="dv")
                    wait("sync", "dv", devac[d])
                    so_cnt[j] = dma("sync", out3[:, d, :], OUTB[j], f"so{j}")
                    outb_free[j] = so_cnt[j]
            prev_guevac = guevac
            prev_devac15 = devac[C - 1]

        wait("sync", "so0", so_cnt[0])
        wait("sync", "so1", so_cnt[1])

        with nc.Block() as block:
            @block.sync
            def _(e):
                for fn in SCHED["sync"]:
                    fn(e)

            @block.gpsimd
            def _(e):
                for fn in SCHED["gpsimd"]:
                    fn(e)

            @block.vector
            def _(e):
                for fn in SCHED["vector"]:
                    fn(e)

            @block.scalar
            def _(e):
                for fn in SCHED["scalar"]:
                    fn(e)

            @block.tensor
            def _(e):
                for fn in SCHED["tensor"]:
                    fn(e)

    return nc


def kernel(**inputs):
    x = np.asarray(inputs["L0_attn_x"], dtype=np.float32)

    def att_w(w):
        a = np.asarray(w, dtype=np.float32).reshape(C, P, C, 128)
        return np.ascontiguousarray(a.transpose(2, 1, 0, 3).reshape(C, P, 2048))

    def ffn_w(w):
        a = np.asarray(w, dtype=np.int32).reshape(HC, 128, C, P)
        return np.ascontiguousarray(a.transpose(0, 3, 2, 1).reshape(HC, P, 2048))

    def down_w(w):
        a = np.asarray(w, dtype=np.int32).reshape(C, 128, NSEG, SEGH, P)
        return np.ascontiguousarray(
            a.transpose(2, 0, 4, 3, 1).reshape(NSEG, C, P, SEGH * 128))

    w_in = att_w(inputs["L0_attn_W_in"])
    w_dec = att_w(inputs["L0_attn_W_decay"])
    w_gate = att_w(inputs["L0_attn_W_gate"])
    w_out = att_w(inputs["L0_attn_W_out"])
    wg_t = ffn_w(inputs["L0_ffn_wq_gate"])
    wu_t = ffn_w(inputs["L0_ffn_wq_up"])
    wd_t = down_w(inputs["L0_ffn_wq_down"])

    def rs(v):
        return np.asarray(v, dtype=np.float32).reshape(-1, 128).T

    smalls = np.ascontiguousarray(np.concatenate(
        [rs(inputs["L0_attn_b_in"]), rs(inputs["L0_attn_b_decay"]),
         rs(inputs["L0_attn_decay_bias"]), rs(inputs["L0_attn_b_gate"]),
         rs(inputs["L0_attn_b_out"]), rs(inputs["L0_ffn_s_down"]),
         rs(inputs["L0_ffn_s_gate"]), rs(inputs["L0_ffn_s_up"])], axis=1))

    xT = np.ascontiguousarray(np.transpose(x, (0, 2, 1)))

    nc = build_program()
    in_maps = []
    for core in range(8):
        b, q = core // 4, core % 4
        qs = np.zeros((P, 4), dtype=np.float32)
        qs[:, q] = 1.0
        in_maps.append(dict(
            xq=np.ascontiguousarray(xT[b][:, q * T:(q + 1) * T]),
            w_in=w_in, w_dec=w_dec, w_gate=w_gate, w_out=w_out,
            wg_t=wg_t, wu_t=wu_t, wd_t=wd_t, smalls=smalls, qsel=qs))

    res = run_bass_kernel_spmd(nc, in_maps, core_ids=list(range(8)))

    outT = np.empty((2, D, 4096), dtype=np.float32)
    for core in range(8):
        b, q = core // 4, core % 4
        outT[b][:, q * T:(q + 1) * T] = np.asarray(res.results[core]["outT"])
    return np.ascontiguousarray(np.transpose(outT, (0, 2, 1)))


# version 11
# speedup vs baseline: 1.1205x; 1.0193x over previous
import numpy as np
import concourse.bass as bass
import concourse.mybir as mybir
from concourse.bass_utils import run_bass_kernel_spmd
from contextlib import ExitStack

F32 = mybir.dt.float32
BF16 = mybir.dt.bfloat16
I32 = mybir.dt.int32
AF = mybir.ActivationFunctionType
ALU = mybir.AluOpType

P = 128
T = 1024
C = 16
D = 2048
HID = 5632
HC = 44
SEGH = 22
NSEG = 2
NB = 512
EPSV = 1e-5

OFF_SM = 0
OFF_SM2 = 736
OFF_QS = 800
OFF_MBOX = 816
OFF_MG = 944
OFF_CARRY = 1456
OFF_CT1 = 1520
OFF_CT2 = 1584
OFF_ONESR = 1648
OFF_ONESRB = 2160
OFF_ONESC = 2416
OFF_ONESCB = 2420
OFF_EPS = 2424
OFF_CROWD = 4480
OFF_H = 6656
OFF_BIG = 39424
OFF_WS = 137728
OFF_A1 = 162304
OFF_A2 = 166400
OFF_A3 = 170496
OFF_A4 = 174592
OFF_A5 = 178688
OFF_A6 = 182784
OFF_A7 = 186880
TOTAL_BYTES = 207360

SM_BIN = 0
SM_BDEC = 16
SM_DBIAS = 32
SM_BGATE = 48
SM_BOUT = 64
SM_SDOWN = 80
SM_SGATE = 96
SM_SUP = 140


def build_program():
    nc = bass.Bass("TRN2", target_bir_lowering=False)

    xq = nc.dram_tensor("xq", [D, T], F32, kind="ExternalInput").ap()
    w_in = nc.dram_tensor("w_in", [C, P, 2048], F32, kind="ExternalInput").ap()
    w_dec = nc.dram_tensor("w_dec", [C, P, 2048], F32, kind="ExternalInput").ap()
    w_gate = nc.dram_tensor("w_gate", [C, P, 2048], F32, kind="ExternalInput").ap()
    w_out = nc.dram_tensor("w_out", [C, P, 2048], F32, kind="ExternalInput").ap()
    wg_t = nc.dram_tensor("wg_t", [HC, P, 2048], I32, kind="ExternalInput").ap()
    wu_t = nc.dram_tensor("wu_t", [HC, P, 2048], I32, kind="ExternalInput").ap()
    wd_t = nc.dram_tensor("wd_t", [NSEG, C, P, SEGH * 128], I32,
                          kind="ExternalInput").ap()
    smalls = nc.dram_tensor("smalls", [P, 184], F32, kind="ExternalInput").ap()
    qsel = nc.dram_tensor("qsel", [P, 4], F32, kind="ExternalInput").ap()
    outT = nc.dram_tensor("outT", [D, T], F32, kind="ExternalOutput").ap()
    cc_in = nc.dram_tensor("cc_in", [P, 32], F32)
    cc_out = nc.dram_tensor("cc_out", [4 * P, 32], F32)

    xq3 = xq.rearrange("(c p) t -> p c t", p=P)
    out3 = outT.rearrange("(c p) t -> p c t", p=P)
    ccg = cc_out.rearrange("(r p) j -> p r j", p=P)

    with ExitStack() as ctx:
        AR = ctx.enter_context(nc.sbuf_tensor("AR", [P, TOTAL_BYTES // 2], BF16))

        def cv_f32(off_b, rows, cols):
            v = AR[:, off_b // 2:(off_b + cols * 4) // 2].bitcast(F32)
            return v[0:rows, :] if rows != P else v

        def cv_bf(off_b, rows, cols):
            v = AR[:, off_b // 2:(off_b + cols * 2) // 2]
            return v[0:rows, :] if rows != P else v

        SM = cv_f32(OFF_SM, P, 184)
        SM2 = cv_f32(OFF_SM2, P, 16)
        QS = cv_f32(OFF_QS, P, 4)
        MBOX = cv_f32(OFF_MBOX, P, 32)
        MG = cv_f32(OFF_MG, P, 128)
        CARRY = cv_f32(OFF_CARRY, P, 16)
        CT1 = cv_f32(OFF_CT1, P, 16)
        CT2 = cv_f32(OFF_CT2, P, 16)
        ONESR = cv_f32(OFF_ONESR, 1, 128)
        ONESRB = cv_bf(OFF_ONESRB, 1, 128)
        ONESCB = cv_bf(OFF_ONESCB, P, 1)
        EPS = cv_f32(OFF_EPS, 1, 1)
        CROWD = cv_bf(OFF_CROWD, 1, 1024)
        H = cv_bf(OFF_H, P, C * T)
        XN = cv_f32(OFF_BIG, P, C * T)

        def z1t(c):
            return cv_bf(OFF_BIG + c * 2048, P, T)

        def z2t(c):
            return cv_bf(OFF_BIG + 32768 + c * 2048, P, T)

        def yt(c):
            return cv_bf(OFF_BIG + 65536 + c * 2048, P, T)

        def gut(j):
            if j < 16:
                return yt(j)
            return cv_bf(OFF_A7 + (j - 16) * 2048, P, T)

        WS = [cv_bf(OFF_WS + j * 4096, P, 2048) for j in range(6)]
        WD = [cv_bf(OFF_A3, P, SEGH * 128), cv_bf(OFF_A5, P, SEGH * 128)]
        XS = [cv_f32(OFF_A1, P, T), cv_f32(OFF_A2, P, T), cv_f32(OFF_A4, P, T)]
        OUTB = XS
        SQB = [cv_bf(OFF_A3, P, T), cv_bf(OFF_A3 + 2048, P, T),
               cv_bf(OFF_A4, P, T), cv_bf(OFF_A4 + 2048, P, T)]
        ABC = cv_f32(OFF_A3, P, T)
        BBC = cv_f32(OFF_A4, P, T)
        SCRD = cv_f32(OFF_A3, P, T)
        SCRB = cv_f32(OFF_A4, P, T)
        SCRT = cv_f32(OFF_A5, P, T)
        ZEROS = cv_f32(OFF_A6, P, T)
        STATES = cv_f32(OFF_A5, P, T)
        GPS = cv_f32(OFF_A7, P, T)
        T1 = cv_f32(OFF_A7, 1, 1024)
        T2 = cv_f32(OFF_A7 + 4096, 1, 1024)
        ROW3 = cv_f32(OFF_A7 + 8192, 1, 1024)
        AROW = cv_f32(OFF_A7 + 12288, 1, 1024)
        BROW = cv_f32(OFF_A7 + 16384, 1, 1024)
        GTMP = cv_f32(OFF_A7 + 12288, P, T)
        CROWDBC = GTMP
        DSCR = cv_f32(OFF_A7 + 16384, P, T)

        PS = [ctx.enter_context(nc.psum_tensor(f"PS_{i}", [P, 1024], F32))
              for i in range(4)]

        sem_names = (["ld", "mb", "cc", "mg", "pe", "dv", "ac", "gp",
                      "xs0", "xs1", "xs2", "wd0", "wd1", "so0", "so1"]
                     + [f"ws{j}" for j in range(6)])
        SEMS = {n: ctx.enter_context(nc.semaphore(n)) for n in sem_names}
        CNT = {n: 0 for n in sem_names}

        SCHED = {"sync": [], "gpsimd": [], "vector": [], "scalar": [],
                 "tensor": []}

        def emit(eng, fn):
            SCHED[eng].append(fn)

        def wait(eng, sem, val):
            if val <= 0:
                return
            emit(eng, lambda e, h=SEMS[sem], v=val: e.wait_ge(h, v))

        def op(eng, meth, *args, inc=None, incby=1, **kw):
            nosync = kw.pop("nosync", False)
            if eng in ("vector", "gpsimd") and meth != "dma_start":
                esem = "dv" if eng == "vector" else "gp"
                if not nosync:
                    wait(eng, esem, CNT[esem])
                if inc is None:
                    inc, incby = esem, 1
            if inc:
                CNT[inc] += incby
            h = SEMS[inc] if inc else None
            def fn(e, meth=meth, args=args, kw=kw, h=h, incby=incby):
                r = getattr(e, meth)(*args, **kw)
                if h is not None:
                    r.then_inc(h, incby)
            emit(eng, fn)
            return CNT[inc] if inc else None

        def dma(queue, out, in_, sem):
            return op(queue, "dma_start", inc=sem, incby=16, out=out, in_=in_)

        ws_free = [("pe", 0)] * 6
        ws_ready = [0] * 6

        def load_stripe(j, src_ap):
            s, v = ws_free[j]
            wait("gpsimd", s, v)
            ws_ready[j] = dma("gpsimd", WS[j][:, :], src_ap, f"ws{j}")

        dma("sync", SM, smalls, "ld")
        dma("sync", QS, qsel, "ld")
        op("vector", "memset", ONESR, 1.0)
        op("vector", "memset", ONESRB, 1.0)
        op("vector", "memset", ONESCB, 1.0)
        op("vector", "memset", EPS, EPSV)
        op("vector", "memset", ZEROS, 0.0)
        wait("vector", "ld", 32)
        op("vector", "tensor_tensor", SM2, SM[:, 16:32], SM[:, 32:48], ALU.add)

        ph0_done = [0] * C
        hcast = [0] * C
        sq_done = [0] * C
        for c in range(C):
            hcast[c] = dma("gpsimd", H[:, c * T:(c + 1) * T], xq3[:, c, :],
                           "xs0")
        for c in range(C):
            wait("scalar", "xs0", hcast[c])
            if c >= 4:
                wait("scalar", "pe", ph0_done[c - 4])
            sq_done[c] = op("scalar", "activation", SQB[c % 4],
                            H[:, c * T:(c + 1) * T], AF.Square,
                            bias=0.0, scale=1.0, inc="ac")
            wait("tensor", "xs0", hcast[c])
            for n in range(2):
                op("tensor", "matmul", PS[0][0:1, n * NB:(n + 1) * NB], ONESCB,
                   H[:, c * T + n * NB:c * T + n * NB + NB], start=(c == 0),
                   stop=(c == C - 1))
            wait("tensor", "ac", sq_done[c])
            for n in range(2):
                ph0_done[c] = op("tensor", "matmul",
                                 PS[2][0:1, n * NB:(n + 1) * NB], ONESCB,
                                 SQB[c % 4][:, n * NB:(n + 1) * NB],
                                 start=(c == 0), stop=(c == C - 1),
                                 inc="pe" if n == 1 else None)

        def stats_rows(src_s1, src_s2):
            n1 = op("vector", "tensor_scalar", T1, src_s1, 1.0 / D, None,
                    ALU.mult, inc="dv")
            n2 = op("vector", "tensor_scalar", T2, src_s2, 1.0 / D, None,
                    ALU.mult, inc="dv")
            wait("vector", "dv", n1)
            n3 = op("vector", "tensor_tensor", ROW3, T1, T1, ALU.mult,
                    inc="dv")
            wait("vector", "dv", n3)
            dvn = op("vector", "tensor_tensor", T2, T2, ROW3, ALU.subtract,
                     inc="dv")
            wait("scalar", "dv", dvn)
            acn = op("scalar", "activation", ROW3, T2, AF.Sqrt, bias=EPS,
                     scale=1.0, inc="ac")
            wait("vector", "ac", acn)
            n5 = op("vector", "reciprocal", AROW, ROW3, inc="dv")
            wait("vector", "dv", n5)
            return op("vector", "scalar_tensor_tensor", BROW, T1, -1.0, AROW,
                      ALU.mult, ALU.mult, inc="dv")

        def broadcasts_and_copies(rows_n):
            wait("tensor", "dv", rows_n)
            for n in range(2):
                op("tensor", "matmul", PS[1][:, n * NB:(n + 1) * NB], ONESR,
                   AROW[0:1, n * NB:(n + 1) * NB], start=True, stop=True)
            bc = 0
            for n in range(2):
                bc = op("tensor", "matmul", PS[3][:, n * NB:(n + 1) * NB],
                        ONESR, BROW[0:1, n * NB:(n + 1) * NB], start=True,
                        stop=True, inc="pe" if n == 1 else None)
            wait("vector", "pe", bc)
            op("vector", "tensor_copy", ABC, PS[1][:, :])
            return op("vector", "tensor_copy", BBC, PS[3][:, :], inc="dv")

        wait("vector", "pe", ph0_done[C - 1])
        rows1 = stats_rows(PS[0][0:1, :], PS[2][0:1, :])
        abc1 = broadcasts_and_copies(rows1)

        load_stripe(0, w_in[0])
        load_stripe(1, w_dec[0])

        h1dv = [0] * C
        h1gp = [0] * C
        wait("vector", "dv", abc1)
        for c in range(C):
            hc = H[:, c * T:(c + 1) * T]
            if c % 3 == 2:
                if c == 2:
                    wait("gpsimd", "dv", abc1)
                else:
                    wait("gpsimd", "gp", CNT["gp"])
                n = op("gpsimd", "tensor_tensor", GPS, hc, ABC, ALU.mult,
                       inc="gp")
                wait("gpsimd", "gp", n)
                h1gp[c] = op("gpsimd", "tensor_tensor", hc, GPS, BBC,
                             ALU.add, inc="gp")
            else:
                n = op("vector", "tensor_tensor", SCRT, hc, ABC, ALU.mult,
                       inc="dv")
                wait("vector", "dv", n)
                h1dv[c] = op("vector", "tensor_tensor", hc, SCRT, BBC,
                             ALU.add, inc="dv")

        def wait_h1(eng, c):
            if c % 3 == 2:
                wait(eng, "gp", h1gp[c])
            else:
                wait(eng, "dv", h1dv[c])

        X1B = XS[0]
        pe_inp = [0] * C
        pe_tile = [0] * C
        sig_inp = [0] * C
        act_tile = [0] * C
        stt_inp = [0] * C
        dv_tile = [0] * C
        for c in range(C):
            ji, jd = (2 * c) % 6, (2 * c + 1) % 6
            if c >= 1:
                load_stripe(ji, w_in[c])
                load_stripe(jd, w_dec[c])
            p0, p1 = (PS[0], PS[1]) if c % 2 == 0 else (PS[2], PS[3])
            wait("tensor", f"ws{ji}", ws_ready[ji])
            wait("tensor", f"ws{jd}", ws_ready[jd])
            if c >= 2:
                wait("tensor", "ac", act_tile[c - 2])
                wait("tensor", "dv", stt_inp[c - 2])
            for n in range(2):
                for k in range(C):
                    if c == 0 and n == 0:
                        wait_h1("tensor", k)
                    pe_inp[c] = op(
                        "tensor", "matmul", p0[:, n * NB:(n + 1) * NB],
                        WS[ji][:, k * 128:(k + 1) * 128],
                        H[:, k * T + n * NB:k * T + n * NB + NB],
                        start=(k == 0), stop=(k == C - 1),
                        inc="pe" if (n == 1 and k == C - 1) else None)
            for n in range(2):
                for k in range(C):
                    pe_tile[c] = op(
                        "tensor", "matmul", p1[:, n * NB:(n + 1) * NB],
                        WS[jd][:, k * 128:(k + 1) * 128],
                        H[:, k * T + n * NB:k * T + n * NB + NB],
                        start=(k == 0), stop=(k == C - 1),
                        inc="pe" if (n == 1 and k == C - 1) else None)
            ws_free[ji] = ("pe", pe_tile[c])
            ws_free[jd] = ("pe", pe_tile[c])

            if c >= 1:
                wait("scalar", "dv", dv_tile[c - 1])
            wait("scalar", "pe", pe_inp[c])
            sig_inp[c] = op("scalar", "activation", SCRB, p0[:, :],
                            AF.Sigmoid,
                            bias=SM[:, SM_BIN + c:SM_BIN + c + 1], scale=1.0,
                            inc="ac")
            wait("scalar", "pe", pe_tile[c])
            act_tile[c] = op("scalar", "activation", SCRD, p1[:, :],
                             AF.Sigmoid, bias=SM2[:, c:c + 1], scale=1.0,
                             inc="ac")

            wait("vector", "ac", sig_inp[c])
            stt_inp[c] = op("vector", "scalar_tensor_tensor", SCRT, p0[:, :],
                            SM[:, SM_BIN + c:SM_BIN + c + 1], SCRB,
                            ALU.add, ALU.mult, inc="dv")
            wait("vector", "ac", act_tile[c])
            op("vector", "tensor_tensor", SCRB, SCRD, SCRT, ALU.mult)
            op("vector", "tensor_tensor", SCRT, SCRT, SCRB, ALU.subtract)
            op("vector", "tensor_tensor_scan", SCRB, SCRD, SCRT, 0.0,
               ALU.mult, ALU.add, inc="dv")
            op("vector", "tensor_tensor_scan", SCRT, SCRD, ZEROS, 1.0,
               ALU.mult, ALU.add, inc="dv")
            op("vector", "tensor_copy", MBOX[:, c:c + 1], SCRB[:, T - 1:T])
            op("vector", "tensor_copy", MBOX[:, 16 + c:17 + c],
               SCRT[:, T - 1:T])
            op("vector", "tensor_copy", z1t(c), SCRB)
            dv_tile[c] = op("vector", "tensor_copy", z2t(c), SCRT, inc="dv")

        for c0 in (0, 1):
            load_stripe((32 + c0) % 6, w_gate[c0])
        wait("sync", "dv", dv_tile[C - 1])
        mb = dma("sync", cc_in.ap(), MBOX, "mb")
        wait("gpsimd", "mb", mb)
        op("gpsimd", "collective_compute", "AllGather", ALU.bypass,
           replica_groups=[[0, 1, 2, 3], [4, 5, 6, 7]],
           ins=[cc_in.ap().opt()], outs=[cc_out.ap().opt()], inc="cc")
        wait("sync", "cc", CNT["cc"])
        mg = dma("sync", MG.rearrange("p (r j) -> p r j", j=32), ccg, "mg")
        wait("vector", "mg", mg)
        op("vector", "memset", CARRY, 0.0)
        op("vector", "memset", CT1, 0.0)
        for r in range(3):
            op("vector", "tensor_tensor", CT2, MG[:, r * 32 + 16:r * 32 + 32],
               CT1, ALU.mult)
            op("vector", "tensor_tensor", CT1, MG[:, r * 32:r * 32 + 16], CT2,
               ALU.add)
            op("vector", "scalar_tensor_tensor", CARRY, CT1,
               QS[:, r + 1:r + 2], CARRY, ALU.mult, ALU.add)

        pe_gate = [0] * C
        gate_evac = [0] * C
        corr = [None] * C
        for c in range(C):
            jg = (32 + c) % 6
            if c >= 2:
                load_stripe(jg, w_gate[c])
            pg = PS[0] if c % 2 == 0 else PS[2]
            wait("tensor", f"ws{jg}", ws_ready[jg])
            if c >= 2:
                wait("tensor", "ac", gate_evac[c - 2])
            for n in range(2):
                for k in range(C):
                    pe_gate[c] = op(
                        "tensor", "matmul", pg[:, n * NB:(n + 1) * NB],
                        WS[jg][:, k * 128:(k + 1) * 128],
                        H[:, k * T + n * NB:k * T + n * NB + NB],
                        start=(k == 0), stop=(k == C - 1),
                        inc="pe" if (n == 1 and k == C - 1) else None)
            ws_free[jg] = ("pe", pe_gate[c])

            wait("scalar", "pe", pe_gate[c])
            gate_evac[c] = op("scalar", "activation", yt(c), pg[:, :],
                              AF.Sigmoid,
                              bias=SM[:, SM_BGATE + c:SM_BGATE + c + 1],
                              scale=1.0, inc="ac")

            wait("vector", "ac", gate_evac[c])
            op("vector", "scalar_tensor_tensor", STATES, z2t(c),
               CARRY[:, c:c + 1], z1t(c), ALU.mult, ALU.add)
            n = op("vector", "tensor_tensor", yt(c), yt(c), STATES, ALU.mult)
            corr[c] = ("dv", n)

        pe_out = [0] * C
        d_evac = [0] * C
        xs_ready = [0, 0]
        cast2_done = [0] * C
        sq2_done = [0] * C
        e1_done = [0] * C
        for e in range(C):
            jo = (48 + e) % 6
            load_stripe(jo, w_out[e])
            j = e % 2
            if e >= 2:
                wait("sync", "dv", d_evac[e - 2])
            xs_ready[j] = dma("sync", XS[j], xq3[:, e, :], f"xs{j}")

            po = PS[0] if e % 2 == 0 else PS[2]
            wait("tensor", f"ws{jo}", ws_ready[jo])
            if e >= 2:
                wait("tensor", "dv", d_evac[e - 2])
            for n in range(2):
                for c in range(C):
                    if e == 0 and n == 0:
                        wait("tensor", corr[c][0], corr[c][1])
                    pe_out[e] = op(
                        "tensor", "matmul", po[:, n * NB:(n + 1) * NB],
                        WS[jo][:, c * 128:(c + 1) * 128],
                        yt(c)[:, n * NB:(n + 1) * NB],
                        start=(c == 0), stop=(c == C - 1),
                        inc="pe" if (n == 1 and c == C - 1) else None)
            ws_free[jo] = ("pe", pe_out[e])

            wait("vector", "pe", pe_out[e])
            wait("vector", f"xs{j}", xs_ready[j])
            d_evac[e] = op("vector", "scalar_tensor_tensor",
                           XN[:, e * T:(e + 1) * T], po[:, :],
                           SM[:, SM_BOUT + e:SM_BOUT + e + 1], XS[j],
                           ALU.add, ALU.add, inc="dv")
            cast2_done[e] = op("vector", "tensor_copy",
                               H[:, e * T:(e + 1) * T],
                               XN[:, e * T:(e + 1) * T], inc="dv")
            wait("scalar", "dv", cast2_done[e])
            if e >= 4:
                wait("scalar", "pe", e1_done[e - 4])
            sq2_done[e] = op("scalar", "activation", SQB[e % 4],
                             H[:, e * T:(e + 1) * T], AF.Square, bias=0.0,
                             scale=1.0, inc="ac")

            def emit_stats(e2):
                j2 = e2 % 2
                wait("tensor", "dv", cast2_done[e2])
                for n in range(2):
                    op("tensor", "matmul", PS[1][0:1, n * NB:(n + 1) * NB],
                       ONESCB, H[:, e2 * T + n * NB:e2 * T + n * NB + NB],
                       start=(e2 == 0), stop=(e2 == C - 1))
                wait("tensor", "ac", sq2_done[e2])
                for n in range(2):
                    e1_done[e2] = op("tensor", "matmul",
                                     PS[3][0:1, n * NB:(n + 1) * NB], ONESCB,
                                     SQB[e2 % 4][:, n * NB:(n + 1) * NB],
                                     start=(e2 == 0), stop=(e2 == C - 1),
                                     inc="pe" if n == 1 else None)

            if e >= 1:
                emit_stats(e - 1)
            if e == C - 1:
                emit_stats(e)

        wait("vector", "pe", e1_done[C - 1])
        rows2 = stats_rows(PS[1][0:1, :], PS[3][0:1, :])
        abc2 = broadcasts_and_copies(rows2)

        for h0 in (0, 1):
            load_stripe((2 * h0) % 6, wg_t[h0])
            load_stripe((2 * h0 + 1) % 6, wu_t[h0])

        h2dv = [0] * C
        h2gp = [0] * C
        wait("vector", "dv", abc2)
        for c in range(C):
            hc = H[:, c * T:(c + 1) * T]
            xc = XN[:, c * T:(c + 1) * T]
            if c % 3 == 2:
                if c == 2:
                    wait("gpsimd", "dv", abc2)
                n = op("gpsimd", "tensor_tensor", GPS, xc, ABC, ALU.mult)
                h2gp[c] = op("gpsimd", "tensor_tensor", hc, GPS, BBC,
                             ALU.add, inc="gp")
            else:
                n = op("vector", "tensor_tensor", SCRT, xc, ABC, ALU.mult)
                h2dv[c] = op("vector", "tensor_tensor", hc, SCRT, BBC,
                             ALU.add, inc="dv")

        def wait_h2(eng, c):
            if c % 3 == 2:
                wait(eng, "gp", h2gp[c])
            else:
                wait(eng, "dv", h2dv[c])

        wd_free = [("dv", h2dv[15]), ("dv", h2dv[15])]
        wd_ready = [0, 0]
        so_cnt = [0, 0]
        outb_free = [0, 0]
        prev_devac15 = 0
        prev_guevac = None
        for seg in range(NSEG):
            g_done = [0] * SEGH
            u_done = [0] * SEGH
            gevac = [0] * SEGH
            guevac = [0] * SEGH
            for jj in range(SEGH):
                h = seg * SEGH + jj
                jg, ju = (2 * h) % 6, (2 * h + 1) % 6
                if not (seg == 0 and jj < 2):
                    load_stripe(jg, wg_t[h])
                    load_stripe(ju, wu_t[h])
                pg = PS[0] if jj % 2 == 0 else PS[2]
                pu = PS[1] if jj % 2 == 0 else PS[3]
                wait("tensor", f"ws{jg}", ws_ready[jg])
                wait("tensor", f"ws{ju}", ws_ready[ju])
                if seg == 0 and jj == 0:
                    wait("tensor", "dv", abc2)
                if seg == 1 and jj == 0:
                    wait("tensor", "dv", prev_devac15)
                if jj >= 2:
                    wait("tensor", "ac", gevac[jj - 2])
                    wait("tensor", "dv", guevac[jj - 2])
                for n in range(2):
                    for k in range(C):
                        if seg == 0 and jj == 0 and n == 0:
                            wait_h2("tensor", k)
                        g_done[jj] = op(
                            "tensor", "matmul", pg[:, n * NB:(n + 1) * NB],
                            WS[jg][:, k * 128:(k + 1) * 128],
                            H[:, k * T + n * NB:k * T + n * NB + NB],
                            start=(k == 0), stop=(k == C - 1),
                            inc="pe" if (n == 1 and k == C - 1) else None)
                for n in range(2):
                    for k in range(C):
                        u_done[jj] = op(
                            "tensor", "matmul", pu[:, n * NB:(n + 1) * NB],
                            WS[ju][:, k * 128:(k + 1) * 128],
                            H[:, k * T + n * NB:k * T + n * NB + NB],
                            start=(k == 0), stop=(k == C - 1),
                            inc="pe" if (n == 1 and k == C - 1) else None)
                ws_free[jg] = ("pe", u_done[jj])
                ws_free[ju] = ("pe", u_done[jj])

                wait("scalar", "pe", g_done[jj])
                if jj >= 1:
                    wait("scalar", "dv", guevac[jj - 1])
                elif seg == 1:
                    wait("scalar", "dv", prev_devac15)
                gevac[jj] = op("scalar", "activation", GTMP, pg[:, :],
                               AF.Silu, bias=0.0,
                               scale=SM[:, SM_SGATE + h:SM_SGATE + h + 1],
                               inc="ac")

                wait("vector", "pe", u_done[jj])
                wait("vector", "ac", gevac[jj])
                guevac[jj] = op("vector", "scalar_tensor_tensor", gut(jj),
                                pu[:, :], SM[:, SM_SUP + h:SM_SUP + h + 1],
                                GTMP, ALU.mult, ALU.mult, inc="dv")

            wait("tensor", "dv", guevac[SEGH - 1])
            csg = 0
            for jj in range(SEGH):
                for n in range(2):
                    csg = op("tensor", "matmul",
                             PS[1][0:1, n * NB:(n + 1) * NB], ONESCB,
                             gut(jj)[:, n * NB:(n + 1) * NB],
                             start=(jj == 0), stop=(jj == SEGH - 1),
                             inc="pe" if (jj == SEGH - 1 and n == 1) else None)
            wait("vector", "pe", csg)
            crw = op("vector", "tensor_scalar", CROWD, PS[1][0:1, :],
                     -128.0, None, ALU.mult, inc="dv")
            wait("tensor", "dv", crw)
            csg2 = 0
            for n in range(2):
                csg2 = op("tensor", "matmul", PS[3][:, n * NB:(n + 1) * NB],
                          ONESRB, CROWD[0:1, n * NB:(n + 1) * NB], start=True,
                          stop=True, inc="pe" if n == 1 else None)
            wait("vector", "pe", csg2)
            crowd_done = op("vector", "tensor_copy", CROWDBC, PS[3][:, :],
                            inc="dv")

            down_done = [0] * C
            devac = [0] * C
            for d in range(C):
                j = d % 2
                s, v = wd_free[j]
                wait("gpsimd", s, v)
                wd_ready[j] = dma("gpsimd", WD[j][:, :], wd_t[seg, d],
                                  f"wd{j}")

                pd = PS[0] if d % 2 == 0 else PS[2]
                wait("tensor", f"wd{j}", wd_ready[j])
                if d == 0:
                    wait("tensor", "dv", crowd_done)
                if d >= 2:
                    wait("tensor", "dv", devac[d - 2])
                for n in range(2):
                    for jj in range(SEGH):
                        down_done[d] = op(
                            "tensor", "matmul", pd[:, n * NB:(n + 1) * NB],
                            WD[j][:, jj * 128:(jj + 1) * 128],
                            gut(jj)[:, n * NB:(n + 1) * NB],
                            start=(jj == 0), stop=(jj == SEGH - 1),
                            inc="pe" if (n == 1 and jj == SEGH - 1) else None)
                wd_free[j] = ("pe", down_done[d])

                wait("vector", "pe", down_done[d])
                op("vector", "tensor_tensor", DSCR, pd[:, :], CROWDBC,
                   ALU.add)
                if seg == 0:
                    devac[d] = op("vector", "scalar_tensor_tensor",
                                  XN[:, d * T:(d + 1) * T], DSCR,
                                  SM[:, SM_SDOWN + d:SM_SDOWN + d + 1],
                                  XN[:, d * T:(d + 1) * T], ALU.mult, ALU.add,
                                  inc="dv")
                else:
                    wait("vector", f"so{j}", outb_free[j])
                    devac[d] = op("vector", "scalar_tensor_tensor", OUTB[j],
                                  DSCR,
                                  SM[:, SM_SDOWN + d:SM_SDOWN + d + 1],
                                  XN[:, d * T:(d + 1) * T], ALU.mult, ALU.add,
                                  inc="dv")
                    wait("sync", "dv", devac[d])
                    so_cnt[j] = dma("sync", out3[:, d, :], OUTB[j], f"so{j}")
                    outb_free[j] = so_cnt[j]
            prev_guevac = guevac
            prev_devac15 = devac[C - 1]

        wait("sync", "so0", so_cnt[0])
        wait("sync", "so1", so_cnt[1])

        with nc.Block() as block:
            @block.sync
            def _(e):
                for fn in SCHED["sync"]:
                    fn(e)

            @block.gpsimd
            def _(e):
                for fn in SCHED["gpsimd"]:
                    fn(e)

            @block.vector
            def _(e):
                for fn in SCHED["vector"]:
                    fn(e)

            @block.scalar
            def _(e):
                for fn in SCHED["scalar"]:
                    fn(e)

            @block.tensor
            def _(e):
                for fn in SCHED["tensor"]:
                    fn(e)

    return nc


def kernel(**inputs):
    x = np.asarray(inputs["L0_attn_x"], dtype=np.float32)

    def att_w(w):
        a = np.asarray(w, dtype=np.float32).reshape(C, P, C, 128)
        return np.ascontiguousarray(a.transpose(2, 1, 0, 3).reshape(C, P, 2048))

    def ffn_w(w):
        a = np.asarray(w, dtype=np.int32).reshape(HC, 128, C, P)
        return np.ascontiguousarray(a.transpose(0, 3, 2, 1).reshape(HC, P, 2048))

    def down_w(w):
        a = np.asarray(w, dtype=np.int32).reshape(C, 128, NSEG, SEGH, P)
        return np.ascontiguousarray(
            a.transpose(2, 0, 4, 3, 1).reshape(NSEG, C, P, SEGH * 128))

    w_in = att_w(inputs["L0_attn_W_in"])
    w_dec = att_w(inputs["L0_attn_W_decay"])
    w_gate = att_w(inputs["L0_attn_W_gate"])
    w_out = att_w(inputs["L0_attn_W_out"])
    wg_t = ffn_w(inputs["L0_ffn_wq_gate"])
    wu_t = ffn_w(inputs["L0_ffn_wq_up"])
    wd_t = down_w(inputs["L0_ffn_wq_down"])

    def rs(v):
        return np.asarray(v, dtype=np.float32).reshape(-1, 128).T

    smalls = np.ascontiguousarray(np.concatenate(
        [rs(inputs["L0_attn_b_in"]), rs(inputs["L0_attn_b_decay"]),
         rs(inputs["L0_attn_decay_bias"]), rs(inputs["L0_attn_b_gate"]),
         rs(inputs["L0_attn_b_out"]), rs(inputs["L0_ffn_s_down"]),
         rs(inputs["L0_ffn_s_gate"]), rs(inputs["L0_ffn_s_up"])], axis=1))

    xT = np.ascontiguousarray(np.transpose(x, (0, 2, 1)))

    nc = build_program()
    in_maps = []
    for core in range(8):
        b, q = core // 4, core % 4
        qs = np.zeros((P, 4), dtype=np.float32)
        qs[:, q] = 1.0
        in_maps.append(dict(
            xq=np.ascontiguousarray(xT[b][:, q * T:(q + 1) * T]),
            w_in=w_in, w_dec=w_dec, w_gate=w_gate, w_out=w_out,
            wg_t=wg_t, wu_t=wu_t, wd_t=wd_t, smalls=smalls, qsel=qs))

    res = run_bass_kernel_spmd(nc, in_maps, core_ids=list(range(8)))

    outT = np.empty((2, D, 4096), dtype=np.float32)
    for core in range(8):
        b, q = core // 4, core % 4
        outT[b][:, q * T:(q + 1) * T] = np.asarray(res.results[core]["outT"])
    return np.ascontiguousarray(np.transpose(outT, (0, 2, 1)))
